# version 15
# baseline (speedup 1.0000x reference)
import numpy as np
import ml_dtypes
import concourse.bass as bass
import concourse.mybir as mybir
from concourse.bass_utils import run_bass_kernel_spmd
from contextlib import ExitStack

F32 = mybir.dt.float32
BF16 = mybir.dt.bfloat16
AF = mybir.ActivationFunctionType
ALU = mybir.AluOpType
AX = mybir.AxisListType

B_LOC = 2
L = 2048
T = B_LOC * L
D = 1024
NCORES = 8
NT = T // 128



class Tok:
    __slots__ = ("sem", "val")

    def __init__(self, sem, val):
        self.sem, self.val = sem, val


class Em:

    def __init__(self, ctx, ename, sems):
        self.ctx, self.ename = ctx, ename
        self.sems = sems
        self.cnt = [0] * len(sems)
        self.rr = 0
        self.ops = []
        self.waited = {}

    def raw(self, meth, *args, **kwargs):
        self.ops.append([meth, args, kwargs, []])
        return self

    def inc(self, rotate=False):
        if rotate:
            self.rr = (self.rr + 1) % len(self.sems)
        i = self.rr
        self.cnt[i] += 1
        assert self.ops and self.ops[-1][0] != "WAIT"
        self.ops[-1][3].append((self.sems[i], 1))
        return Tok(self.sems[i], self.cnt[i])

    def tok(self):
        return Tok(self.sems[self.rr], self.cnt[self.rr])

    def wait(self, tok):
        if tok is None:
            return
        k = id(tok.sem)
        if self.waited.get(k, -1) >= tok.val:
            return
        self.waited[k] = tok.val
        self.ops.append(["WAIT", tok.sem, tok.val, None])

    def flush(self, eng):
        for op in self.ops:
            if op[0] == "WAIT":
                eng.wait_ge(op[1], op[2])
            else:
                inst = getattr(eng, op[0])(*op[1], **op[2])
                for sem, n in op[3]:
                    inst.then_inc(sem, n)

    def drain(self):
        return self.raw("drain")


class DmaQ:

    def __init__(self, ctx, em, nsems, name):
        self.em = em
        self.sems = [ctx.sem(f"{name}{i}") for i in range(nsems)]
        self.cnt = [0] * nsems
        self.rr = -1
        self.cur = None

    def begin(self):
        self.rr = (self.rr + 1) % len(self.sems)
        self.cur = self.rr
        return self

    def dma(self, out, in_, **kw):
        assert self.cur is not None
        self.em.raw("dma_start", out=out, in_=in_, **kw)
        self.em.ops[-1][3].append((self.sems[self.cur], 16))
        self.cnt[self.cur] += 1
        return self

    def end(self):
        t = Tok(self.sems[self.cur], 16 * self.cnt[self.cur])
        self.cur = None
        return t

    def one(self, out, in_, **kw):
        self.begin()
        self.dma(out, in_, **kw)
        return self.end()


class Ctx:
    def __init__(self, nc, es):
        self.nc, self.es = nc, es
        self.te = Em(self, "tensor", [self.sem("s_te")])
        self.dve = Em(self, "vector", [self.sem("s_dve")])
        self.act = Em(self, "scalar", [self.sem("s_act")])
        self.pool = Em(self, "gpsimd", [self.sem(f"s_pl{i}") for i in range(4)])
        self.sync = Em(self, "sync", [self.sem("s_sy")])
        self.qa = DmaQ(self, self.sync, 8, "qa")
        self.qb = DmaQ(self, self.sync, 8, "qb")
        self.qw = DmaQ(self, self.pool, 8, "qw")
        self.qp = DmaQ(self, self.pool, 8, "qp")

    def sem(self, name):
        return self.es.enter_context(self.nc.semaphore(name))

    def sb(self, name, shape, dtype, es=None):
        return (es or self.es).enter_context(self.nc.sbuf_tensor(name, shape, dtype))

    def barrier(self, toks):
        for t in (toks if isinstance(toks, (list, tuple)) else [toks]):
            for em in (self.te, self.dve, self.act, self.pool, self.sync):
                em.wait(t)

    def drain_queues(self):
        toks = []
        for q in (self.qa, self.qb, self.qw):
            toks += [Tok(s, 16 * n) for s, n in zip(q.sems, q.cnt) if n]
        self.barrier(toks)

    def const_col(self, val):
        if not hasattr(self, "_cc"):
            self._cc = {}
        if val not in self._cc:
            t = self.sb(f"cc{len(self._cc)}", [128, 1], F32)
            self.dve.raw("memset", t[:, :], float(val))
            self.dve.drain()
            tok = self.dve.inc()
            self._cc[val] = (t, tok)
        return self._cc[val]

    def flush(self):
        with self.nc.Block() as blk:
            blk.tensor(lambda e: self.te.flush(e))
            blk.vector(lambda e: self.dve.flush(e))
            blk.scalar(lambda e: self.act.flush(e))
            blk.gpsimd(lambda e: self.pool.flush(e))
            blk.sync(lambda e: self.sync.flush(e))



def build_ident(c):
    c.ident16 = c.sb("ident16", [128, 128], BF16)
    c.identf = c.sb("identf", [128, 128], F32)
    iop = c.iotap = c.sb("iotap", [128, 1], F32)
    iof = c.iotaf = c.sb("iotaf", [128, 128], F32)
    c.pool.raw("iota", iop[:, :], [[0, 1]], channel_multiplier=1,
               allow_small_or_imprecise_dtypes=True)
    c.pool.raw("iota", iof[:, :], [[1, 128]], channel_multiplier=0,
               allow_small_or_imprecise_dtypes=True)
    tk = c.pool.inc()
    c.dve.wait(tk)
    c.dve.raw("tensor_scalar", c.identf[:, :], iof[:, :], iop[:, 0:1], None,
              ALU.is_equal)
    c.dve.drain()
    c.dve.raw("tensor_copy", c.ident16[:, :], c.identf[:, :])
    c.dve.drain()
    c.ident_tok = c.dve.inc()
    c.te.wait(c.ident_tok)
    c.const_col(1e-5)
    c.const_col(1e-6)
    c.const_col(1.0)


def mk_ln_scratch(c, tag, es=None):
    return {
        "stats": c.sb(f"lns_st{tag}", [128, 12], F32, es=es),
        "aggr": c.sb(f"lns_ag{tag}", [128, 2], F32, es=es),
        "srt": c.sb(f"lns_sr{tag}", [128, 1], F32, es=es),
        "rstd": c.sb(f"lns_rs{tag}", [128, 1], F32, es=es),
        "mrs": c.sb(f"lns_mr{tag}", [128, 1], F32, es=es),
        "t_act": None, "t_dve": None,
    }


def ln_stats(c, x_ap, sc, eps=1e-5):
    c.dve.wait(sc["t_act"])
    F = x_ap.shape[-1]
    ng = (F + 511) // 512
    for g in range(ng):
        c.dve.raw("bn_stats", sc["stats"][:, 6 * g:6 * (g + 1)],
                  x_ap[:, 512 * g:min(512 * (g + 1), F)])
    c.dve.drain()
    c.dve.raw("bn_aggr", sc["aggr"][:, :], sc["stats"][:, 0:6 * ng])
    t_ag = c.dve.inc()
    c.act.wait(t_ag)
    c.act.wait(sc["t_dve"])
    epst, epstok = c.const_col(eps)
    c.act.wait(epstok)
    c.act.raw("activation", sc["srt"][:, 0:1], sc["aggr"][:, 1:2], AF.Sqrt,
              bias=epst[:, 0:1], scale=1.0)
    sc["t_act"] = c.act.inc()
    c.dve.wait(sc["t_act"])
    c.dve.raw("reciprocal", sc["rstd"][:, 0:1], sc["srt"][:, 0:1])
    c.dve.drain()
    c.dve.raw("tensor_tensor", sc["mrs"][:, 0:1], sc["aggr"][:, 0:1],
              sc["rstd"][:, 0:1], ALU.mult)
    c.dve.drain()
    sc["t_dve"] = c.dve.inc()
    return sc["aggr"][:, 0:1], sc["rstd"][:, 0:1], sc["mrs"][:, 0:1]


def ln_stats2(c, x_ap, sc, eps, scale=1.0):
    c.dve.wait(sc["t_act"])
    F = x_ap.shape[-1]
    ng = (F + 511) // 512
    for g in range(ng):
        c.dve.raw("bn_stats", sc["stats"][:, 6 * g:6 * (g + 1)],
                  x_ap[:, 512 * g:min(512 * (g + 1), F)])
    c.dve.drain()
    c.dve.raw("bn_aggr", sc["aggr"][:, :], sc["stats"][:, 0:6 * ng])
    c.dve.drain()
    c.dve.raw("tensor_tensor", sc["mrs"][:, 0:1], sc["aggr"][:, 0:1],
              sc["aggr"][:, 0:1], ALU.mult)
    c.dve.drain()
    c.dve.raw("tensor_tensor", sc["mrs"][:, 0:1], sc["mrs"][:, 0:1],
              sc["aggr"][:, 1:2], ALU.add)
    t_ms = c.dve.inc()
    c.act.wait(t_ms)
    c.act.wait(sc["t_dve"])
    epst, epstok = c.const_col(eps)
    c.act.wait(epstok)
    c.act.raw("activation", sc["srt"][:, 0:1], sc["mrs"][:, 0:1], AF.Sqrt,
              bias=epst[:, 0:1], scale=float(scale))
    sc["t_act"] = c.act.inc()
    c.dve.wait(sc["t_act"])
    c.dve.raw("reciprocal", sc["rstd"][:, 0:1], sc["srt"][:, 0:1])
    c.dve.drain()
    sc["t_dve"] = c.dve.inc()
    return None, sc["rstd"][:, 0:1], None



def stage_A(c, x_in, x_out, W, ses):
    nc = c.nc
    w_vT = [c.sb(f"a_wvT{k}", [128, 1024], BF16, es=ses) for k in range(8)]
    w_outT = [c.sb(f"a_woutT{k}", [128, 1024], BF16, es=ses) for k in range(8)]
    bvcols = c.sb("a_bvcols", [128, 8], F32, es=ses)
    bout_bc = c.sb("a_boutbc", [128, 1024], F32, es=ses)
    v_fm = [c.sb(f"a_vfm{k}", [128, T], BF16, es=ses) for k in range(8)]

    c.qw.begin()
    for k in range(8):
        c.qw.dma(w_vT[k][:, :], W["w_vT"][128 * k:128 * (k + 1), :])
    for k in range(8):
        c.qw.dma(w_outT[k][:, :], W["w_outT"][128 * k:128 * (k + 1), :])
    t_w = c.qw.end()
    c.qa.begin()
    for j in range(8):
        c.qa.dma(bvcols[:, j:j + 1],
                 W["b_v"][128 * j:128 * (j + 1)].rearrange("(p o) -> p o", o=1))
    c.qa.dma(bout_bc[:, :],
             W["b_out"].rearrange("(o f) -> o f", o=1).partition_broadcast(128))
    t_bc = c.qa.end()

    NB = 3
    xb = [c.sb(f"a_xb{i}", [128, 1024], F32, es=ses) for i in range(NB)]
    t1b = [c.sb(f"a_t1b{i}", [128, 1024], BF16, es=ses) for i in range(NB)]
    lns = [mk_ln_scratch(c, f"A{i}", es=ses) for i in range(NB)]
    t1_es = ExitStack()
    t1_fm = [c.sb(f"a_t1fm{k}", [128, T], BF16, es=t1_es) for k in range(8)]

    with nc.psum_tensor("a_pstr", [128, 2048], BF16) as ps_tr:
        btok = [None] * NB
        half_tok = [None, None]
        for i in range(NT):
            bi = i % NB
            c.sync.wait(btok[bi])
            tl = c.qb.one(xb[bi][:, :], x_in[128 * i:128 * (i + 1), :])
            c.dve.wait(tl)
            mean, rstd, mrs = ln_stats(c, xb[bi][:, :], lns[bi])
            c.dve.raw("tensor_scalar", t1b[bi][:, :], xb[bi][:, :],
                      mean, rstd, ALU.subtract, ALU.mult)
            c.dve.drain()
            t_t1 = c.dve.inc()
            hf = i % 2
            c.te.wait(half_tok[hf])
            c.te.wait(t_t1)
            for k in range(8):
                c.te.raw("transpose",
                         ps_tr[:, hf * 1024 + k * 128:hf * 1024 + (k + 1) * 128],
                         t1b[bi][:, k * 128:(k + 1) * 128], c.ident16[:, :])
            t_tr = c.te.inc()
            c.dve.wait(t_tr)
            for k in range(8):
                c.dve.raw("tensor_copy", t1_fm[k][:, 128 * i:128 * (i + 1)],
                          ps_tr[:, hf * 1024 + k * 128:hf * 1024 + (k + 1) * 128])
            btok[bi] = half_tok[hf] = c.dve.inc()
        for t in btok:
            c.te.wait(t)

    with ExitStack() as ps_es:
        ps_q = [ps_es.enter_context(nc.psum_tensor(f"a_psq{i}", [128, 512], F32))
                for i in range(4)]
        c.te.wait(t_w)
        bank_tok = [None] * 4
        for j in range(8):
            for n in range(T // 512):
                pb = n % 4
                c.te.wait(bank_tok[pb])
                for k in range(8):
                    c.te.raw("matmul", ps_q[pb][:, :],
                             w_vT[k][:, j * 128:(j + 1) * 128],
                             t1_fm[k][:, n * 512:(n + 1) * 512],
                             start=(k == 0), stop=(k == 7))
                t_mm = c.te.inc()
                c.dve.wait(t_mm)
                c.dve.wait(t_bc)
                c.dve.raw("tensor_scalar", v_fm[j][:, n * 512:(n + 1) * 512],
                          ps_q[pb][:, :], bvcols[:, j:j + 1], None, ALU.add)
                bank_tok[pb] = c.dve.inc()
        t_v = bank_tok[(T // 512 - 1) % 4]
        for t in bank_tok:
            c.te.wait(t)
    t1_es.close()
    c.barrier([t for t in bank_tok if t is not None])

    with ExitStack() as ps_es:
        ps_w = [ps_es.enter_context(nc.psum_tensor(f"a_psw{i}", [128, 512], F32))
                for i in range(4)]
        a1b = [c.sb(f"a_a1b{i}", [128, 1024], F32, es=ps_es) for i in range(NB)]
        x1b = [c.sb(f"a_x1b{i}", [128, 1024], F32, es=ps_es) for i in range(NB)]
        lns2 = [mk_ln_scratch(c, f"A2{i}", es=ps_es) for i in range(NB)]
        otoks = [None] * NB
        bank_tok = [None] * 4
        for i in range(NT):
            bi = i % NB
            c.sync.wait(otoks[bi])
            tl = c.qb.one(xb[bi][:, :], x_in[128 * i:128 * (i + 1), :])
            for jc in range(2):
                pb = (2 * i + jc) % 4
                c.te.wait(bank_tok[pb])
                for k in range(8):
                    c.te.raw("matmul", ps_w[pb][:, :],
                             v_fm[k][:, 128 * i:128 * (i + 1)],
                             w_outT[k][:, jc * 512:(jc + 1) * 512],
                             start=(k == 0), stop=(k == 7))
                t_mm = c.te.inc()
                c.dve.wait(t_mm)
                c.dve.raw("tensor_tensor", a1b[bi][:, jc * 512:(jc + 1) * 512],
                          ps_w[pb][:, :], bout_bc[:, jc * 512:(jc + 1) * 512],
                          ALU.add)
                bank_tok[pb] = c.dve.inc()
            c.dve.drain()
            mean, rstd, mrs = ln_stats(c, a1b[bi][:, :], lns2[bi])
            c.dve.wait(tl)
            c.dve.raw("tensor_scalar", x1b[bi][:, :], xb[bi][:, :], mrs, None,
                      ALU.subtract)
            c.dve.drain()
            c.dve.raw("scalar_tensor_tensor", x1b[bi][:, :], a1b[bi][:, :],
                      rstd, x1b[bi][:, :], ALU.mult, ALU.add)
            c.dve.drain()
            t_x1 = c.dve.inc()
            c.sync.wait(t_x1)
            otoks[bi] = c.qb.one(x_out[128 * i:128 * (i + 1), :], x1b[bi][:, :])
    return [t for t in otoks if t is not None]




def stage_weights_bf16(c, specs):
    toks = {}
    for src_ap, dst_ap, nrows, ncols in specs:
        for r0 in range(0, nrows, 2048):
            rw = min(2048, nrows - r0)
            t = c.qw.one(dst_ap[r0:r0 + rw, :], src_ap[r0:r0 + rw, :])
            toks[id(t.sem)] = t
    return list(toks.values())


def load_cols(c, q, dst, src_ap, n, tag=None):
    q.begin()
    for j in range(n):
        q.dma(dst[:, j:j + 1],
              src_ap[128 * j:128 * (j + 1)].rearrange("(p o) -> p o", o=1))
    return q.end()


def entry_tile(c, x_row_ap, xb, tb, lns, fm_tiles, col0, ps_tr, hf, eps=1e-5):
    tl = c.qb.one(xb[:, :], x_row_ap)
    c.dve.wait(tl)
    mean, rstd, mrs = ln_stats(c, xb[:, :], lns, eps=eps)
    c.dve.raw("tensor_scalar", tb[:, :], xb[:, :], mean, rstd,
              ALU.subtract, ALU.mult)
    c.dve.drain()
    t_t1 = c.dve.inc()
    c.te.wait(t_t1)
    nk = tb.shape[1] // 128
    for k in range(nk):
        c.te.raw("transpose",
                 ps_tr[:, hf * 1024 + k * 128:hf * 1024 + (k + 1) * 128],
                 tb[:, k * 128:(k + 1) * 128], c.ident16[:, :])
    t_tr = c.te.inc()
    c.dve.wait(t_tr)
    for k in range(nk):
        c.dve.raw("tensor_copy", fm_tiles[k][:, col0:col0 + 128],
                  ps_tr[:, hf * 1024 + k * 128:hf * 1024 + (k + 1) * 128])
    return c.dve.inc()


def stage_B(c, x_in, x_out, W, W16, ses):
    nc = c.nc
    CH = 512
    NCHUNK = T // CH
    scols = {}
    for nm in ("w1_s", "w2_s", "h1_s", "h2_s"):
        scols[nm] = c.sb(f"b_{nm}c", [128, 32], F32, es=ses)
        load_cols(c, c.qa, scols[nm], W[nm], 32)
    s3bc = c.sb("b_s3bc", [128, 1024], F32, es=ses)
    t_s3 = c.qa.one(s3bc[:, :],
                    W["w3_s"].rearrange("(o f) -> o f", o=1).partition_broadcast(128))
    col_toks = [Tok(s, 16 * n) for s, n in zip(c.qa.sems, c.qa.cnt) if n]
    c.barrier(col_toks + [t_s3])

    NB = 2
    xb = [c.sb(f"b_xb{i}", [128, 1024], F32, es=ses) for i in range(NB)]
    tb = [c.sb(f"b_tb{i}", [128, 1024], BF16, es=ses) for i in range(NB)]
    lns = [mk_ln_scratch(c, f"B{i}", es=ses) for i in range(NB)]
    lns2 = [mk_ln_scratch(c, f"B2{i}", es=ses) for i in range(NB)]
    a1b = [c.sb(f"b_a1b{i}", [128, 1024], F32, es=ses) for i in range(4)]
    x1b = [c.sb(f"b_x1b{i}", [128, 1024], F32, es=ses) for i in range(NB)]
    u1sb = [c.sb(f"b_u1{i}", [128, CH], BF16, es=ses) for i in range(2)]
    hA = [c.sb(f"b_hA{i}", [128, CH], BF16, es=ses) for i in range(32)]
    hB = [c.sb(f"b_hB{i}", [128, CH], BF16, es=ses) for i in range(32)]
    t2_fm = [c.sb(f"b_t2fm{k}", [128, CH], BF16, es=ses) for k in range(8)]
    wbuf = [c.sb(f"b_wb{s}", [128, 32 * 512], BF16, es=ses) for s in range(2)]
    wq = [[wbuf[s][:, 512 * k:512 * (k + 1)] for k in range(32)]
          for s in range(2)]
    wv = [[wbuf[s][:, 1024 * k:1024 * (k + 1)] for k in range(8)]
          for s in range(2)]

    otoks = [None] * NB
    entry_tok = [None] * NB
    prev_chunk_tok = None
    wv_tok = [None, None]
    wq_tok = [None, None]
    with ExitStack() as ps_es:
        ps_tr = ps_es.enter_context(nc.psum_tensor("b_pstr", [128, 2048], BF16))
        ps = [ps_es.enter_context(nc.psum_tensor(f"b_ps{i}", [128, 512], F32))
              for i in range(4)]
        bank_tok = [None] * 4
        for ci in range(NCHUNK):
            for ti in range(4):
                i = 4 * ci + ti
                bi = i % NB
                c.sync.wait(entry_tok[bi])
                c.sync.wait(otoks[bi])
                hf = i % 2
                if prev_chunk_tok is not None:
                    c.te.wait(prev_chunk_tok)
                entry_tok[bi] = entry_tile(
                    c, x_in[128 * i:128 * (i + 1), :], xb[bi], tb[bi],
                    lns[bi], t2_fm, 128 * ti, ps_tr, hf)
            t_entry = entry_tok[(4 * ci + 3) % NB]
            for t in entry_tok:
                c.te.wait(t)
            for q in range(8):
                s = q % 2
                c.sync.wait(wv_tok[s])
                c.qa.begin()
                for k in range(8):
                    c.qa.dma(wv[s][k][:, 0:512],
                             W16["w1T"][128 * k:128 * (k + 1),
                                        512 * q:512 * (q + 1)])
                    c.qa.dma(wv[s][k][:, 512:1024],
                             W16["w2T"][128 * k:128 * (k + 1),
                                        512 * q:512 * (q + 1)])
                t_wvq = c.qa.end()
                c.te.wait(t_wvq)
                for jj in range(4):
                    j = 4 * q + jj
                    pa, pb_ = 2 * (jj % 2), 2 * (jj % 2) + 1
                    c.te.wait(bank_tok[pa])
                    for k in range(8):
                        c.te.raw("matmul", ps[pa][:, :],
                                 wv[s][k][:, jj * 128:(jj + 1) * 128],
                                 t2_fm[k][:, :], start=(k == 0), stop=(k == 7))
                    t_mma = c.te.inc()
                    c.te.wait(bank_tok[pb_])
                    for k in range(8):
                        c.te.raw("matmul", ps[pb_][:, :],
                                 wv[s][k][:, 512 + jj * 128:512 + (jj + 1) * 128],
                                 t2_fm[k][:, :], start=(k == 0), stop=(k == 7))
                    t_mmb = c.te.inc()
                    ui = jj % 2
                    c.act.wait(t_mma)
                    c.act.wait(bank_tok[pa])
                    c.act.raw("activation", u1sb[ui][:, :], ps[pa][:, :],
                              AF.Silu, bias=0.0, scale=scols["w1_s"][:, j:j + 1])
                    t_u1 = c.act.inc()
                    c.dve.wait(t_u1)
                    c.dve.wait(t_mmb)
                    c.dve.raw("scalar_tensor_tensor", hA[j][:, :], ps[pb_][:, :],
                              scols["w2_s"][:, j:j + 1], u1sb[ui][:, :],
                              ALU.mult, ALU.mult)
                    bank_tok[pa] = bank_tok[pb_] = c.dve.inc()
                wv_tok[s] = bank_tok[pa]
            for t_ in bank_tok:
                c.te.wait(t_)
            wq_tok[0] = wq_tok[1] = bank_tok[pa]
            for phase, (src_t, dst_t, sc_nm, w16nm) in enumerate(
                    ((hA, hB, "h1_s", "h1T"), (hB, hA, "h2_s", "h2T"))):
                for q in range(8):
                    s = q % 2
                    c.sync.wait(wq_tok[s])
                    c.qa.begin()
                    for k in range(32):
                        c.qa.dma(wq[s][k][:, :],
                                 W16[w16nm][128 * k:128 * (k + 1),
                                            512 * q:512 * (q + 1)])
                    t_wqq = c.qa.end()
                    c.te.wait(t_wqq)
                    for jj in range(4):
                        j = 4 * q + jj
                        pb_ = jj % 4
                        c.te.wait(bank_tok[pb_])
                        for k in range(32):
                            c.te.raw("matmul", ps[pb_][:, :],
                                     wq[s][k][:, jj * 128:(jj + 1) * 128],
                                     src_t[k][:, :], start=(k == 0), stop=(k == 31))
                        t_mm = c.te.inc()
                        c.act.wait(t_mm)
                        c.act.raw("activation", dst_t[j][:, :], ps[pb_][:, :],
                                  AF.Gelu, bias=0.0,
                                  scale=scols[sc_nm][:, j:j + 1])
                        bank_tok[pb_] = c.act.inc()
                    wq_tok[s] = bank_tok[(4 * q + 3) % 4]
                for t_ in bank_tok:
                    c.te.wait(t_)
            a1_tok = None
            for jc in range(2):
                s = jc
                c.sync.wait(wq_tok[s])
                c.qa.begin()
                for k in range(32):
                    c.qa.dma(wq[s][k][:, :],
                             W16["w3T"][128 * k:128 * (k + 1),
                                        512 * jc:512 * (jc + 1)])
                t_w3q = c.qa.end()
                c.te.wait(t_w3q)
                for ti in range(4):
                    pb_ = ti % 4
                    c.te.wait(bank_tok[pb_])
                    for k in range(32):
                        c.te.raw("matmul", ps[pb_][:, :],
                                 hA[k][:, ti * 128:(ti + 1) * 128],
                                 wq[s][k][:, :], start=(k == 0), stop=(k == 31))
                    t_mm = c.te.inc()
                    c.dve.wait(t_mm)
                    c.dve.wait(t_s3)
                    c.dve.raw("tensor_tensor",
                              a1b[ti][:, jc * 512:(jc + 1) * 512],
                              ps[pb_][:, :], s3bc[:, jc * 512:(jc + 1) * 512],
                              ALU.mult)
                    bank_tok[pb_] = c.dve.inc()
                wq_tok[s] = bank_tok[3]
            c.dve.drain()
            a1_tok = bank_tok[3]
            wv_tok[0] = wv_tok[1] = a1_tok
            for ti in range(4):
                i = 4 * ci + ti
                bi = i % NB
                c.dve.wait(otoks[bi])
                tl = c.qb.one(xb[bi][:, :], x_in[128 * i:128 * (i + 1), :])
                mean, rstd, mrs = ln_stats(c, a1b[ti][:, :], lns2[ti % NB])
                c.dve.wait(tl)
                c.dve.raw("tensor_scalar", x1b[bi][:, :], xb[bi][:, :], mrs,
                          None, ALU.subtract)
                c.dve.drain()
                c.dve.raw("scalar_tensor_tensor", x1b[bi][:, :], a1b[ti][:, :],
                          rstd, x1b[bi][:, :], ALU.mult, ALU.add)
                c.dve.drain()
                t_x1 = c.dve.inc()
                c.sync.wait(t_x1)
                otoks[bi] = c.qb.one(x_out[128 * i:128 * (i + 1), :],
                                     x1b[bi][:, :])
            prev_chunk_tok = a1_tok
    return [t for t in otoks if t is not None]




def stage_D(c, x_in, x_out, W, W16, ses):
    nc = c.nc
    CH = 512
    NCHUNK = T // CH
    tok_d = nc.dram_tensor("d_tok", [8, 128, T], BF16, kind="Internal").ap()
    t4_d = nc.dram_tensor("d_t4", [T, 1024], F32, kind="Internal").ap()
    nwbc = c.sb("d_nwbc", [128, 1024], F32, es=ses)
    t_nw = c.qa.one(nwbc[:, :],
                    W["d_nw"].rearrange("(o f) -> o f", o=1).partition_broadcast(128))
    gate_sb = [c.sb(f"d_gate{k}", [128, 8], BF16, es=ses) for k in range(8)]
    c.qw.begin()
    for k in range(8):
        c.qw.dma(gate_sb[k][:, :], W["d_gateT"][128 * k:128 * (k + 1), :])
    t_gw = c.qw.end()
    b2sb = c.sb("d_b2sb", [8, 1024], BF16, es=ses)
    t_b2 = c.qw.one(b2sb[:, :], W["d_b2"][:, :])
    b1c = c.sb("d_b1c", [128, 128], F32, es=ses)
    c.qa.begin()
    for e in range(8):
        for j in range(16):
            c.qa.dma(b1c[:, 16 * e + j:16 * e + j + 1],
                     W["d_b1"][e, 128 * j:128 * (j + 1)].rearrange("(p o) -> p o", o=1))
    t_b1 = c.qa.end()
    zeros8 = c.sb("d_z8", [128, 8], F32, es=ses)
    c.dve.raw("memset", zeros8[:, :], 0.0)
    c.dve.drain()
    t_z8 = c.dve.inc()
    c.barrier([t_nw, t_gw, t_b2, t_b1, t_z8])

    comb = [c.sb(f"d_cmb{i}", [128, 8], F32, es=ses) for i in range(NT)]
    cmbT = c.sb("d_cmbT", [8, T], BF16, es=ses)
    gsb = c.sb("d_gsb", [128, 8], F32, es=ses)
    r_sc = {nm: c.sb(f"d_r{nm}", [128, lw], F32, es=ses) for nm, lw in
            (("m1n", 1), ("m2", 1), ("eq", 8), ("cum", 8), ("c1e", 8),
             ("fst2", 8), ("a", 1), ("den", 1), ("w1r", 1), ("w2r", 1),
             ("msk", 8))}

    with ExitStack() as pes:
        xb = [c.sb(f"d_xb{i}", [128, 1024], F32, es=pes) for i in range(2)]
        tb = [c.sb(f"d_tb{i}", [128, 1024], BF16, es=pes) for i in range(2)]
        t4b = [c.sb(f"d_t4b{i}", [128, 1024], F32, es=pes) for i in range(2)]
        lns = [mk_ln_scratch(c, f"D{i}", es=pes) for i in range(2)]
        tfm = [c.sb(f"d_tfm{i}", [128, 8 * 128], BF16, es=pes) for i in range(2)]
        ps_tr = pes.enter_context(nc.psum_tensor("d_pstr", [128, 2048], BF16))
        ps_g = pes.enter_context(nc.psum_tensor("d_psg", [128, 8], F32))
        ps_ct = pes.enter_context(nc.psum_tensor("d_psct", [8, 128], F32))
        ent_free = [None, None]
        half_tok = [None, None]
        tfm_free = [None, None]
        t4_tok = [None, None]
        g_tok = None
        ct_tok = None
        prev_a = None
        prev_cmb = None
        for i in range(NT):
            bi = i % 2
            c.sync.wait(ent_free[bi])
            c.sync.wait(t4_tok[bi])
            tl = c.qb.one(xb[bi][:, :], x_in[128 * i:128 * (i + 1), :])
            d = c.dve
            d.wait(tl)
            mean, rstd, mrs = ln_stats(c, xb[bi][:, :], lns[bi])
            d.raw("tensor_scalar", t4b[bi][:, :], xb[bi][:, :],
                  mean, rstd, ALU.subtract, ALU.mult)
            d.drain()
            t_t4 = d.inc()
            c.sync.wait(t_t4)
            t4_tok[bi] = c.qb.one(t4_d[128 * i:128 * (i + 1), :], t4b[bi][:, :])
            _, rstd2, _ = ln_stats2(c, t4b[bi][:, :], lns[bi], eps=1e-6)
            d.raw("scalar_tensor_tensor", tb[bi][:, :], t4b[bi][:, :],
                  rstd2, nwbc[:, :], ALU.mult, ALU.mult)
            d.drain()
            t_tb = d.inc()
            ent_free[bi] = t_tb
            hf = i % 2
            c.te.wait(t_tb)
            c.te.wait(half_tok[hf])
            for k in range(8):
                c.te.raw("transpose",
                         ps_tr[:, hf * 1024 + k * 128:hf * 1024 + (k + 1) * 128],
                         tb[bi][:, k * 128:(k + 1) * 128], c.ident16[:, :])
            t_tr = c.te.inc()
            d.wait(t_tr)
            d.wait(tfm_free[bi])
            for k in range(8):
                d.raw("tensor_copy", tfm[bi][:, 128 * k:128 * (k + 1)],
                      ps_tr[:, hf * 1024 + k * 128:hf * 1024 + (k + 1) * 128])
            d.drain()
            t_ev = d.inc()
            half_tok[hf] = t_ev
            c.sync.wait(t_ev)
            tfm_free[bi] = c.qa.one(
                tok_d[:, :, 128 * i:128 * (i + 1)].rearrange("a p t -> p a t"),
                tfm[bi][:, :].rearrange("p (a t) -> p a t", t=128))
            c.te.wait(t_ev)
            c.te.wait(g_tok)
            for k in range(8):
                c.te.raw("matmul", ps_g[:, :], tfm[bi][:, 128 * k:128 * (k + 1)],
                         gate_sb[k][:, :], start=(k == 0), stop=(k == 7))
            t_g = c.te.inc()
            d.wait(t_g)
            d.wait(prev_a)
            d.raw("tensor_copy", gsb[:, :], ps_g[:, :])
            d.raw("reduce_max", r_sc["m1n"][:, :], ps_g[:, :], axis=AX.X,
                  negate=True)
            g_tok = d.inc()
            d.drain()
            d.raw("tensor_scalar", r_sc["eq"][:, :], gsb[:, :],
                  r_sc["m1n"][:, 0:1], -1.0, ALU.add, ALU.mult)
            d.drain()
            d.raw("tensor_scalar", r_sc["eq"][:, :], r_sc["eq"][:, :], 0.0,
                  None, ALU.is_equal)
            d.drain()
            d.raw("tensor_tensor_scan", r_sc["cum"][:, :], r_sc["eq"][:, :],
                  zeros8[:, :], 0.0, ALU.add, ALU.add)
            d.drain()
            d.raw("tensor_scalar", r_sc["c1e"][:, :], r_sc["cum"][:, :],
                  1.0, None, ALU.is_equal)
            d.drain()
            d.raw("tensor_tensor", comb[i][:, :], r_sc["eq"][:, :],
                  r_sc["c1e"][:, :], ALU.mult)
            d.drain()
            d.raw("scalar_tensor_tensor", r_sc["msk"][:, :], comb[i][:, :],
                  -1e30, gsb[:, :], ALU.mult, ALU.add)
            d.drain()
            d.raw("reduce_max", r_sc["m2"][:, :], r_sc["msk"][:, :], axis=AX.X)
            d.drain()
            d.raw("tensor_scalar", r_sc["eq"][:, :], r_sc["msk"][:, :],
                  r_sc["m2"][:, 0:1], None, ALU.is_equal)
            d.drain()
            d.raw("tensor_tensor_scan", r_sc["cum"][:, :], r_sc["eq"][:, :],
                  zeros8[:, :], 0.0, ALU.add, ALU.add)
            d.drain()
            d.raw("tensor_scalar", r_sc["c1e"][:, :], r_sc["cum"][:, :],
                  1.0, None, ALU.is_equal)
            d.drain()
            d.raw("tensor_tensor", r_sc["fst2"][:, :], r_sc["eq"][:, :],
                  r_sc["c1e"][:, :], ALU.mult)
            t_route = d.inc()
            c.act.wait(t_route)
            c.act.wait(prev_cmb)
            c.act.raw("activation", r_sc["a"][:, 0:1], r_sc["m2"][:, 0:1],
                      AF.Exp, bias=r_sc["m1n"][:, 0:1], scale=1.0)
            prev_a = c.act.inc()
            d.wait(prev_a)
            d.raw("tensor_scalar", r_sc["den"][:, :], r_sc["a"][:, :], 1.0,
                  None, ALU.add)
            d.drain()
            d.raw("reciprocal", r_sc["w1r"][:, :], r_sc["den"][:, :])
            d.drain()
            d.raw("tensor_tensor", r_sc["w2r"][:, :], r_sc["a"][:, :],
                  r_sc["w1r"][:, :], ALU.mult)
            d.drain()
            d.raw("tensor_scalar", comb[i][:, :], comb[i][:, :],
                  r_sc["w1r"][:, 0:1], None, ALU.mult)
            d.drain()
            d.raw("scalar_tensor_tensor", comb[i][:, :], r_sc["fst2"][:, :],
                  r_sc["w2r"][:, 0:1], comb[i][:, :], ALU.mult, ALU.add)
            d.drain()
            prev_cmb = d.inc()
            c.te.wait(prev_cmb)
            c.te.wait(ct_tok)
            c.te.raw("transpose", ps_ct[:, :], comb[i][:, :], c.identf[:, :])
            t_ctm = c.te.inc()
            d.wait(t_ctm)
            d.raw("tensor_copy", cmbT[:, 128 * i:128 * (i + 1)], ps_ct[0:8, :])
            d.drain()
            ct_tok = d.inc()
        c.barrier([ct_tok] + [t for t in tfm_free if t] +
                  [t for t in t4_tok if t])

    with ExitStack() as pes:
        tok_fm = [[c.sb(f"d_tok{p}_{k}", [128, CH], BF16, es=pes)
                   for k in range(8)] for p in range(2)]
        acc = [[c.sb(f"d_acc{p}_{i}", [128, 1024], F32, es=pes)
                for i in range(4)] for p in range(2)]
        h_fm = [c.sb(f"d_h{k}", [128, CH], BF16, es=pes) for k in range(16)]
        w1buf = [c.sb(f"d_w1{k}", [128, 2048], BF16, es=pes) for k in range(8)]
        w2buf = [c.sb(f"d_w2{k}", [128, 1024], BF16, es=pes) for k in range(16)]
        xbE = [c.sb(f"d_xbE{i}", [128, 1024], F32, es=pes) for i in range(2)]
        x1b = [c.sb(f"d_x1b{i}", [128, 1024], F32, es=pes) for i in range(2)]
        lns2 = [mk_ln_scratch(c, f"D2{i}", es=pes) for i in range(2)]
        ps = [pes.enter_context(nc.psum_tensor(f"d_ps{i}", [128, 512], F32))
              for i in range(4)]
        bank_tok = [None] * 4
        st = {"w1_tok": None, "w2_tok": None, "otok": [None, None],
              "ld": [None, None], "acc_free": [None, None]}

        def load_chunk(ci):
            p = ci % 2
            c.sync.wait(st["ld"][p])
            c.qa.begin()
            for k in range(8):
                c.qa.dma(tok_fm[p][k][:, :], tok_d[k, :, CH * ci:CH * (ci + 1)])
            t_tok = c.qa.end()
            c.sync.wait(st["acc_free"][p])
            c.qb.begin()
            for ti in range(4):
                i = 4 * ci + ti
                c.qb.dma(acc[p][ti][:, :], t4_d[128 * i:128 * (i + 1), :])
            t_acc = c.qb.end()
            return t_tok, t_acc

        def exit_tile(ci, ti):
            p = ci % 2
            i = 4 * ci + ti
            bi = i % 2
            d = c.dve
            d.wait(st["otok"][bi])
            tl = c.qb.one(xbE[bi][:, :], x_in[128 * i:128 * (i + 1), :])
            mean, rstd, mrs = ln_stats(c, acc[p][ti][:, :], lns2[bi])
            d.wait(tl)
            d.raw("tensor_scalar", x1b[bi][:, :], xbE[bi][:, :], mrs,
                  None, ALU.subtract)
            d.drain()
            d.raw("scalar_tensor_tensor", x1b[bi][:, :], acc[p][ti][:, :],
                  rstd, x1b[bi][:, :], ALU.mult, ALU.add)
            d.drain()
            t_x1 = d.inc()
            c.sync.wait(t_x1)
            st["otok"][bi] = c.qb.one(x_out[128 * i:128 * (i + 1), :],
                                      x1b[bi][:, :])
            if ti == 3:
                st["acc_free"][p] = t_x1
            return st["otok"][bi]

        ld_tok = {0: load_chunk(0)}
        final = []
        for ci in range(NCHUNK):
            p = ci % 2
            t_tok, t_acc = ld_tok.pop(ci)
            c.te.wait(t_tok)
            for e in range(8):
                c.sync.wait(st["w1_tok"])
                c.qa.begin()
                for k in range(8):
                    c.qa.dma(w1buf[k][:, :],
                             W16["w1T"][1024 * e + 128 * k:1024 * e + 128 * (k + 1), :])
                t_w1 = c.qa.end()
                c.sync.wait(st["w2_tok"])
                c.qa.begin()
                for k in range(16):
                    c.qa.dma(w2buf[k][:, :],
                             W16["w2T"][2048 * e + 128 * k:2048 * e + 128 * (k + 1), :])
                t_w2 = c.qa.end()
                c.te.wait(t_w1)
                for j in range(16):
                    pb_ = j % 4
                    c.te.wait(bank_tok[pb_])
                    for k in range(8):
                        c.te.raw("matmul", ps[pb_][:, :],
                                 w1buf[k][:, 128 * j:128 * (j + 1)],
                                 tok_fm[p][k][:, :], start=(k == 0), stop=(k == 7))
                    t_mm = c.te.inc()
                    c.act.wait(t_mm)
                    c.act.raw("activation", h_fm[j][:, :], ps[pb_][:, :],
                              AF.Gelu, bias=b1c[:, 16 * e + j:16 * e + j + 1],
                              scale=1.0)
                    bank_tok[pb_] = c.act.inc()
                st["w1_tok"] = bank_tok[15 % 4]
                c.te.wait(st["w1_tok"])
                c.te.wait(t_w2)
                for ti in range(4):
                    for jc in range(2):
                        pb_ = (2 * ti + jc) % 4
                        c.te.wait(bank_tok[pb_])
                        if e == 0 and ti == 0 and jc == 0:
                            c.dve.wait(t_acc)
                        for k in range(16):
                            c.te.raw("matmul", ps[pb_][:, :],
                                     h_fm[k][:, 128 * ti:128 * (ti + 1)],
                                     w2buf[k][:, 512 * jc:512 * (jc + 1)],
                                     start=(k == 0), stop=(k == 15))
                        t_mm = c.te.inc()
                        c.dve.wait(t_mm)
                        c.dve.raw("scalar_tensor_tensor",
                                  acc[p][ti][:, 512 * jc:512 * (jc + 1)],
                                  ps[pb_][:, :], comb[4 * ci + ti][:, e:e + 1],
                                  acc[p][ti][:, 512 * jc:512 * (jc + 1)],
                                  ALU.mult, ALU.add)
                        c.dve.drain()
                        bank_tok[pb_] = c.dve.inc()
                st["w2_tok"] = bank_tok[(2 * 3 + 1) % 4]
            for ti in range(4):
                for jc in range(2):
                    pb_ = (2 * ti + jc) % 4
                    c.te.wait(bank_tok[pb_])
                    c.te.raw("matmul", ps[pb_][:, :],
                             cmbT[:, 128 * (4 * ci + ti):128 * (4 * ci + ti + 1)],
                             b2sb[:, 512 * jc:512 * (jc + 1)],
                             start=True, stop=True)
                    t_mm = c.te.inc()
                    c.dve.wait(t_mm)
                    c.dve.raw("tensor_tensor",
                              acc[p][ti][:, 512 * jc:512 * (jc + 1)],
                              ps[pb_][:, :],
                              acc[p][ti][:, 512 * jc:512 * (jc + 1)], ALU.add)
                    c.dve.drain()
                    bank_tok[pb_] = c.dve.inc()
            st["ld"][p] = bank_tok[(2 * 3 + 1) % 4]
            for ti in range(4):
                final.append(exit_tile(ci, ti))
            if ci + 1 < NCHUNK:
                ld_tok[ci + 1] = load_chunk(ci + 1)
    return final


def stage_C(c, x_in, x_out, W, ses):
    nc = c.nc
    CH = 512
    NCHUNK = T // CH
    zxb_d = nc.dram_tensor("c_zxb", [34, 128, T], BF16, kind="Internal").ap()
    t3_d = nc.dram_tensor("c_t3", [T, 1024], F32, kind="Internal").ap()
    u_d = nc.dram_tensor("c_u", [T, 1024], F32, kind="Internal").ap()
    y_d = nc.dram_tensor("c_y", [16, 128, T], BF16, kind="Internal").ap()
    s_d = nc.dram_tensor("c_s", [T // 128, 32, 128], F32, kind="Internal").ap()
    exps_d = nc.dram_tensor("c_exps", [T // 128, 32, 128], BF16,
                            kind="Internal").ap()

    dtc = c.sb("c_dtbc", [32, 1], F32, es=ses)
    alc = c.sb("c_alc", [32, 1], F32, es=ses)
    nAc = c.sb("c_nA", [32, 1], F32, es=ses)
    Drow = c.sb("c_Drow", [1, 32], F32, es=ses)
    Dbc = c.sb("c_Dbc", [128, 32], F32, es=ses)
    onesf = c.sb("c_onesf", [1, 128], F32, es=ses)
    ones16 = c.sb("c_ones16", [1, 128], BF16, es=ses)
    tri16 = c.sb("c_tri16", [128, 128], BF16, es=ses)
    n1bc = c.sb("c_n1bc", [128, 1024], F32, es=ses)
    n2bc = c.sb("c_n2bc", [128, 1024], F32, es=ses)
    mwc = c.sb("c_mwc", [128, 16], F32, es=ses)
    mb2bc = c.sb("c_mb2bc", [128, 1024], F32, es=ses)
    mb1c = c.sb("c_mb1c", [128, 32], F32, es=ses)
    wcc = c.sb("c_wcc", [128, 72], F32, es=ses)
    cbc = c.sb("c_cbc", [128, 18], F32, es=ses)


    c.qa.begin()
    c.qa.dma(dtc[:, :], W["c_dtb"].rearrange("(p o) -> p o", o=1))
    c.qa.dma(alc[:, :], W["c_Alog"].rearrange("(p o) -> p o", o=1))
    c.qa.dma(Drow[:, :], W["c_D"].rearrange("(o p) -> o p", o=1))
    c.qa.dma(n1bc[:, :], W["c_n1w"].rearrange("(o f) -> o f", o=1).partition_broadcast(128))
    c.qa.dma(n2bc[:, :], W["c_n2w"].rearrange("(o f) -> o f", o=1).partition_broadcast(128))
    c.qa.dma(mb2bc[:, :], W["c_mb2"].rearrange("(o f) -> o f", o=1).partition_broadcast(128))
    for j in range(16):
        c.qa.dma(mwc[:, j:j + 1], W["c_mnw"][128 * j:128 * (j + 1)].rearrange("(p o) -> p o", o=1))
    for j in range(32):
        c.qa.dma(mb1c[:, j:j + 1], W["c_mb1"][128 * j:128 * (j + 1)].rearrange("(p o) -> p o", o=1))
    for tl in range(18):
        c.qa.dma(wcc[:, 4 * tl:4 * (tl + 1)], W["c_convw"][128 * tl:128 * (tl + 1), :])
        c.qa.dma(cbc[:, tl:tl + 1], W["c_convb"][128 * tl:128 * (tl + 1)].rearrange("(p o) -> p o", o=1))
    t_cst = c.qa.end()
    d = c.dve
    d.raw("memset", onesf[:, :], 1.0)
    d.drain()
    d.raw("tensor_copy", ones16[:, :], onesf[:, :])
    d.raw("tensor_scalar", tri16[:, :], c.iotaf[:, :], c.iotap[:, 0:1],
          None, ALU.is_ge)
    d.drain()
    t_ones = d.inc()
    c.act.wait(t_cst)
    c.act.raw("activation", nAc[:, 0:1], alc[:, 0:1], AF.Exp, bias=0.0, scale=1.0)
    t_na = c.act.inc()
    d.wait(t_na)
    d.raw("tensor_scalar", nAc[:, 0:1], nAc[:, 0:1], -1.0, None, ALU.mult)
    d.drain()
    t_na = d.inc()
    c.barrier([t_cst, t_ones, t_na])

    NB = 2
    s_es = ExitStack()
    s_sb = c.sb("c_ssb", [32, T], F32, es=s_es)
    nst_sb = c.sb("c_nst", [32, T], F32, es=s_es)
    dt_es = ExitStack()
    dt_sb = c.sb("c_dt", [32, T], F32, es=dt_es)
    with ExitStack() as pes:
        inpT = [c.sb(f"c_inpT{k}", [128, 4384], BF16, es=pes) for k in range(8)]
        c.qw.begin()
        for k in range(8):
            c.qw.dma(inpT[k][:, :], W["c_inpT"][128 * k:128 * (k + 1), :])
        t_w = c.qw.end()
        xb = [c.sb(f"c_xb{i}", [128, 1024], F32, es=pes) for i in range(NB)]
        t3b = [c.sb(f"c_t3b{i}", [128, 1024], F32, es=pes) for i in range(NB)]
        nb = [c.sb(f"c_nb{i}", [128, 1024], BF16, es=pes) for i in range(NB)]
        lns = [mk_ln_scratch(c, f"C{i}", es=pes) for i in range(NB)]
        n1_fm = [c.sb(f"c_n1fm{k}", [128, CH], BF16, es=pes) for k in range(8)]
        zsb = [c.sb(f"c_zsb{i}", [128, CH], BF16, es=pes) for i in range(4)]
        ps_tr = pes.enter_context(nc.psum_tensor("c_pstr", [128, 2048], BF16))
        ps = [pes.enter_context(nc.psum_tensor(f"c_ps{i}", [128, 512], F32))
              for i in range(4)]
        entry_tok = [None] * NB
        t3_tok = [None] * NB
        zdma = [None] * 4
        bank_tok = [None] * 4
        prev_chunk = None
        c.te.wait(t_w)
        for ci in range(NCHUNK):
            for ti in range(4):
                i = 4 * ci + ti
                bi = i % NB
                c.sync.wait(entry_tok[bi])
                c.sync.wait(t3_tok[bi])
                tl = c.qb.one(xb[bi][:, :], x_in[128 * i:128 * (i + 1), :])
                c.dve.wait(tl)
                mean, rstd, mrs = ln_stats(c, xb[bi][:, :], lns[bi])
                c.dve.raw("tensor_scalar", t3b[bi][:, :], xb[bi][:, :],
                          mean, rstd, ALU.subtract, ALU.mult)
                c.dve.drain()
                t_t3 = c.dve.inc()
                c.sync.wait(t_t3)
                t3_tok[bi] = c.qb.one(t3_d[128 * i:128 * (i + 1), :],
                                      t3b[bi][:, :])
                _, rstd1, _ = ln_stats2(c, t3b[bi][:, :], lns[bi], eps=1e-6,
                                        scale=1024.0)
                c.dve.raw("scalar_tensor_tensor", nb[bi][:, :], t3b[bi][:, :],
                          rstd1, n1bc[:, :], ALU.mult, ALU.mult)
                c.dve.drain()
                t_nb = c.dve.inc()
                hf = i % 2
                c.te.wait(t_nb)
                c.te.wait(prev_chunk)
                for k in range(8):
                    c.te.raw("transpose",
                             ps_tr[:, hf * 1024 + k * 128:hf * 1024 + (k + 1) * 128],
                             nb[bi][:, k * 128:(k + 1) * 128], c.ident16[:, :])
                t_tr = c.te.inc()
                c.dve.wait(t_tr)
                for k in range(8):
                    c.dve.raw("tensor_copy", n1_fm[k][:, 128 * ti:128 * (ti + 1)],
                              ps_tr[:, hf * 1024 + k * 128:hf * 1024 + (k + 1) * 128])
                entry_tok[bi] = c.dve.inc()
            for t in entry_tok:
                c.te.wait(t)
            for j in range(35):
                jw = 128 if j < 34 else 32
                pb_ = j % 4
                c.te.wait(bank_tok[pb_])
                for k in range(8):
                    c.te.raw("matmul", ps[pb_][0:jw, :],
                             inpT[k][:, 128 * j:128 * j + jw],
                             n1_fm[k][:, :], start=(k == 0), stop=(k == 7))
                t_mm = c.te.inc()
                c.dve.wait(t_mm)
                if j < 34:
                    sj = j % 4
                    c.dve.wait(zdma[sj])
                    c.dve.raw("tensor_copy", zsb[sj][:, :], ps[pb_][:, :])
                    c.dve.drain()
                    bank_tok[pb_] = c.dve.inc()
                    c.sync.wait(bank_tok[pb_])
                    zdma[sj] = c.qa.one(zxb_d[j, :, CH * ci:CH * (ci + 1)],
                                        zsb[sj][:, :])
                else:
                    c.dve.raw("tensor_copy", dt_sb[:, CH * ci:CH * (ci + 1)],
                              ps[pb_][0:32, :])
                    c.dve.drain()
                    bank_tok[pb_] = c.dve.inc()
            prev_chunk = bank_tok[34 % 4]
        c.barrier([t for t in bank_tok if t is not None] +
                  [t for t in zdma if t is not None] +
                  [t for t in t3_tok if t is not None] +
                  [t for t in entry_tok if t is not None])

    one_col, one_tok = c.const_col(1.0)
    c.act.wait(one_tok)
    with ExitStack() as pes:
        ab = c.sb("c_ab", [32, T], F32, es=pes)
        exps16 = c.sb("c_e16", [32, T], BF16, es=pes)
        zeros32 = c.sb("c_z32", [32, 128], F32, es=pes)
        d.raw("tensor_scalar", dt_sb[:, :], dt_sb[:, :], dtc[:, 0:1], None,
              ALU.add)
        d.drain()
        t_ = d.inc()
        c.act.wait(t_)
        c.act.raw("activation", ab[:, :], dt_sb[:, :], AF.Abs, bias=0.0, scale=1.0)
        c.act.drain()
        c.act.raw("activation", ab[:, :], ab[:, :], AF.Exp, bias=0.0, scale=-1.0)
        c.act.drain()
        c.act.raw("activation", ab[:, :], ab[:, :], AF.Ln,
                  bias=one_col[0:32, 0:1], scale=1.0)
        t_ln = c.act.inc()
        d.wait(t_ln)
        d.raw("tensor_scalar", dt_sb[:, :], dt_sb[:, :], 0.0, None, ALU.max)
        d.drain()
        d.raw("tensor_tensor", dt_sb[:, :], dt_sb[:, :], ab[:, :], ALU.add)
        d.drain()
        t_dt = d.inc()
        d.raw("tensor_scalar", ab[:, :], dt_sb[:, :], nAc[:, 0:1], None, ALU.mult)
        d.drain()
        d.raw("memset", zeros32[:, :], 0.0)
        d.drain()
        for cc in range(T // 128):
            d.raw("tensor_tensor_scan", s_sb[:, 128 * cc:128 * (cc + 1)],
                  ab[:, 128 * cc:128 * (cc + 1)], zeros32[:, :], 0.0,
                  ALU.add, ALU.add)
        d.drain()
        t_s = d.inc()
        c.act.wait(t_dt)
        c.act.wait(t_s)
        c.act.raw("activation", nst_sb[:, :], dt_sb[:, :], AF.Ln, bias=0.0,
                  scale=1.0)
        c.act.drain()
        c.act.raw("activation", exps16[:, :], s_sb[:, :], AF.Exp, bias=0.0,
                  scale=1.0)
        t_e = c.act.inc()
        d.wait(t_e)
        d.raw("tensor_tensor", nst_sb[:, :], nst_sb[:, :], s_sb[:, :],
              ALU.subtract)
        d.drain()
        t_nst = d.inc()
        c.sync.wait(t_s)
        c.sync.wait(t_e)
        c.sync.wait(t_nst)
        c.qa.begin()
        c.qa.dma(s_d.rearrange("c h i -> h c i"),
                 s_sb.rearrange("h (c i) -> h c i", i=128))
        c.qa.dma(exps_d.rearrange("c h i -> h c i"),
                 exps16.rearrange("h (c i) -> h c i", i=128))
        t_sd = c.qa.end()
        with nc.psum_tensor("c_psg", [128, 128], F32) as ps_g:
            c.te.wait(t_cst)
            c.te.wait(t_ones)
            c.te.raw("matmul", ps_g[:, 0:32], onesf[:, :], Drow[:, :],
                     start=True, stop=True)
            t_g = c.te.inc()
            d.wait(t_g)
            d.raw("tensor_copy", Dbc[:, :], ps_g[:, 0:32])
            d.drain()
            t_dbc = d.inc()
        c.barrier([t_sd, t_nst, t_dbc])
    dt_es.close()

    with ExitStack() as pes:
        NP = 4
        xcb = [c.sb(f"c_xcb{t}", [128, CH + 3], BF16, es=pes) for t in range(18)]
        cvb = [c.sb(f"c_cvb{i}", [128, CH], F32, es=pes) for i in range(2)]
        xbc = [c.sb(f"c_xbc{t}", [128, CH], BF16, es=pes) for t in range(18)]
        xt_sb = c.sb("c_xt", [128, 2048], BF16, es=pes)
        btm = c.sb("c_btm", [128, 128], BF16, es=pes)
        nstT = c.sb("c_nstT", [128, 32], F32, es=pes)
        wT = c.sb("c_wT", [128, 32], F32, es=pes)
        gam = c.sb("c_gam", [128, 32], F32, es=pes)
        grow = c.sb("c_grow", [1, 32], BF16, es=pes)
        wbuf = c.sb("c_wbuf", [32, 128], F32, es=pes)
        sbc = [c.sb(f"c_sbc{i}", [128, 4096], F32, es=pes) for i in range(2)]
        ebc = [c.sb(f"c_ebc{i}", [128, 4096], BF16, es=pes) for i in range(2)]
        S_all = c.sb("c_S", [128, 2048], F32, es=pes)
        S16 = c.sb("c_S16", [128, 2048], BF16, es=pes)
        CBt = c.sb("c_CBt", [128, 128], BF16, es=pes)
        d2sb = [c.sb(f"c_d2{i}", [128, 128], F32, es=pes) for i in range(NP)]
        L16 = [c.sb(f"c_L16{i}", [128, 128], BF16, es=pes) for i in range(NP)]
        M16 = [c.sb(f"c_M16{i}", [128, 128], BF16, es=pes) for i in range(NP)]
        xw16 = [c.sb(f"c_xw{i}", [128, 64], BF16, es=pes) for i in range(NP)]
        cpp = [c.sb(f"c_cpp{i}", [128, 128], BF16, es=pes) for i in range(NP)]
        ystg = c.sb("c_ystg", [128, 2048], BF16, es=pes)
        ps_t16 = pes.enter_context(nc.psum_tensor("c_pst16", [128, 1024], BF16))
        ps_misc = pes.enter_context(nc.psum_tensor("c_psmisc", [128, 512], F32))
        ps_cb = ps_misc[:, 96:224]
        ps_y = [pes.enter_context(nc.psum_tensor(f"c_psy{i}", [64, 128], F32))
                for i in range(NP)]
        ps_ds = [pes.enter_context(nc.psum_tensor(f"c_psds{i}", [128, 64], F32))
                 for i in range(2)]

        conv_tok = [None] * 18
        t_d2f = [None] * NP
        t_Lf = [None] * NP
        t_Mf = [None] * NP
        t_yf = [None] * NP
        t_dsf = [None, None]
        t_s16 = None
        t_row = [None, None]
        t_pre = None
        t_ystg = None
        ydma = None
        last_y = None
        for ci in range(NCHUNK):
            first = (ci % 4 == 0)
            for tl in range(18):
                c.sync.wait(conv_tok[tl])
            c.sync.wait(last_y)
            c.qa.begin()
            for tl in range(18):
                if first:
                    c.qa.dma(xcb[tl][:, 3:CH + 3],
                             zxb_d[16 + tl, :, CH * ci:CH * (ci + 1)])
                else:
                    c.qa.dma(xcb[tl][:, 0:CH + 3],
                             zxb_d[16 + tl, :, CH * ci - 3:CH * (ci + 1)])
            t_x = c.qa.end()
            for tl in range(18):
                cb_ = cvb[tl % 2]
                c.dve.wait(t_x)
                c.dve.wait(last_y)
                if tl >= 2:
                    c.dve.wait(conv_tok[tl - 2])
                elif ci > 0:
                    c.dve.wait(conv_tok[16 + tl])
                if first:
                    c.dve.raw("memset", xcb[tl][:, 0:3], 0.0)
                    c.dve.drain()
                c.dve.raw("tensor_scalar", cb_[:, :], xcb[tl][:, 0:CH],
                          wcc[:, 4 * tl:4 * tl + 1], None, ALU.mult)
                c.dve.drain()
                for kk in range(1, 4):
                    c.dve.raw("scalar_tensor_tensor", cb_[:, :],
                              xcb[tl][:, kk:CH + kk],
                              wcc[:, 4 * tl + kk:4 * tl + kk + 1],
                              cb_[:, :], ALU.mult, ALU.add)
                    c.dve.drain()
                t_cv = c.dve.inc()
                c.act.wait(t_cv)
                c.act.wait(conv_tok[tl])
                c.act.wait(last_y)
                c.act.raw("activation", xbc[tl][:, :], cb_[:, :], AF.Silu,
                          bias=cbc[:, tl:tl + 1], scale=1.0)
                conv_tok[tl] = c.act.inc()
            t_conv = conv_tok[17]
            c.te.wait(t_conv)
            for sub in range(4):
                cc = 4 * ci + sub
                lcol = 128 * sub
                cend = 128 * cc + 127
                pi = cc % 2
                c.sync.wait(t_row[pi])
                c.qa.begin()
                c.qa.dma(sbc[pi].rearrange("p (a b) -> p a b", b=128),
                         s_d[cc].partition_broadcast(128))
                c.qa.dma(ebc[pi].rearrange("p (a b) -> p a b", b=128),
                         exps_d[cc].partition_broadcast(128))
                t_rows = c.qa.end()
                if cc % 16 == 0:
                    c.dve.wait(t_s16)
                    c.dve.raw("memset", S_all[:, :], 0.0)
                    c.dve.drain()
                    c.dve.raw("memset", S16[:, :], 0.0)
                    c.dve.drain()
                    t_srst = c.dve.inc()
                    c.te.wait(t_srst)
                    t_s16 = t_srst
                c.te.wait(t_pre)
                c.te.wait(t_s16)
                c.te.raw("transpose", ps_t16[:, 0:128],
                         xbc[16][:, lcol:lcol + 128], c.ident16[:, :])
                t_btr = c.te.inc()
                c.dve.wait(t_btr)
                c.dve.raw("tensor_copy", btm[:, :], ps_t16[:, 0:128])
                c.dve.drain()
                t_btm = c.dve.inc()
                c.te.wait(t_btm)
                for g2 in range(2):
                    for kk in range(8):
                        k = 8 * g2 + kk
                        c.te.raw("transpose", ps_t16[:, kk * 128:(kk + 1) * 128],
                                 xbc[k][:, lcol:lcol + 128], c.ident16[:, :])
                    t_xtr = c.te.inc()
                    c.dve.wait(t_xtr)
                    for kk in range(8):
                        k = 8 * g2 + kk
                        c.dve.raw("tensor_copy", xt_sb[:, 128 * k:128 * (k + 1)],
                                  ps_t16[:, kk * 128:(kk + 1) * 128])
                    c.dve.drain()
                    t_xt = c.dve.inc()
                    if g2 == 0:
                        c.te.wait(t_xt)
                c.te.raw("transpose", ps_misc[:, 0:32],
                         nst_sb[:, 128 * cc:128 * (cc + 1)], c.identf[0:32, 0:32])
                t_ntr = c.te.inc()
                c.dve.wait(t_ntr)
                c.dve.raw("tensor_copy", nstT[:, :], ps_misc[:, 0:32])
                c.dve.drain()
                t_nstT = c.dve.inc()
                c.dve.raw("tensor_scalar", wbuf[:, :],
                          nst_sb[:, 128 * cc:128 * (cc + 1)],
                          s_sb[:, cend:cend + 1], None, ALU.add)
                c.dve.drain()
                t_wb = c.dve.inc()
                c.act.wait(t_wb)
                c.act.raw("activation", wbuf[:, :], wbuf[:, :], AF.Exp,
                          bias=0.0, scale=1.0)
                t_wexp = c.act.inc()
                c.te.wait(t_wexp)
                c.te.wait(t_nstT)
                c.te.raw("transpose", ps_misc[:, 32:64], wbuf[:, :],
                         c.identf[0:32, 0:32])
                t_wtr = c.te.inc()
                c.dve.wait(t_wtr)
                c.dve.raw("tensor_copy", wT[:, :], ps_misc[:, 32:64])
                c.dve.drain()
                t_wT = c.dve.inc()
                c.dve.wait(t_rows)
                c.dve.raw("tensor_copy", grow[0:1, :],
                          ebc[pi].rearrange("p (a b) -> p a b", b=128)[0:1, :, 127])
                c.dve.drain()
                t_grow = c.dve.inc()
                c.te.wait(t_grow)
                c.te.wait(t_wT)
                c.te.raw("matmul", ps_misc[:, 224:256], ones16[:, :],
                         grow[0:1, :], start=True, stop=True)
                t_gb = c.te.inc()
                c.dve.wait(t_gb)
                c.dve.raw("tensor_copy", gam[:, :], ps_misc[:, 224:256])
                c.dve.drain()
                t_gam = c.dve.inc()
                c.te.wait(t_gam)
                c.te.raw("matmul", ps_cb[:, :], xbc[16][:, lcol:lcol + 128],
                         xbc[17][:, lcol:lcol + 128], start=True, stop=True)
                t_cbm = c.te.inc()
                c.dve.wait(t_cbm)
                for t_ in t_Mf:
                    c.dve.wait(t_)
                c.dve.raw("tensor_tensor", CBt[:, :], ps_cb[:, :], tri16[:, :],
                          ALU.mult)
                c.dve.drain()
                t_cbt = c.dve.inc()
                c.te.wait(t_rows)
                c.pool.wait(t_rows)
                c.pool.wait(t_wT)
                def front(h):
                    hp = h % NP
                    dd = c.dve
                    dd.wait(t_nstT)
                    dd.wait(t_Lf[hp])
                    dd.raw("tensor_scalar", d2sb[hp][:, :],
                           sbc[pi][:, 128 * h:128 * (h + 1)],
                           nstT[:, h:h + 1], 30.0, ALU.add, ALU.min)
                    t_d2 = dd.inc()
                    c.act.wait(t_d2)
                    c.act.wait(t_Mf[hp])
                    c.act.raw("activation", L16[hp][:, :], d2sb[hp][:, :],
                              AF.Exp, bias=0.0, scale=1.0)
                    t_L = c.act.inc()
                    t_d2f[hp] = t_L
                    t_Lf[hp] = t_L
                    c.pool.wait(t_dsf[h % 2])
                    c.pool.raw("tensor_scalar", xw16[hp][:, :],
                               xt_sb[:, 64 * h:64 * (h + 1)], wT[:, h:h + 1],
                               None, ALU.mult)
                    t_pl = c.pool.inc(rotate=True)
                    dd.wait(t_yf[hp])
                    dd.raw("tensor_tensor", cpp[hp][:, :],
                           xbc[17][:, lcol:lcol + 128],
                           ebc[pi][:, 128 * h:128 * (h + 1)], ALU.mult)
                    t_cp = dd.inc()
                    return t_L, t_pl, t_cp

                def mid(h, toks):
                    hp = h % NP
                    t_L, t_pl, t_cp = toks
                    dd = c.dve
                    dd.wait(t_L)
                    dd.wait(t_cbt)
                    dd.wait(t_yf[hp])
                    dd.raw("tensor_tensor", M16[hp][:, :], CBt[:, :],
                           L16[hp][:, :], ALU.mult)
                    tM = dd.inc()
                    t_Mf[hp] = tM
                    c.te.wait(tM)
                    c.te.wait(t_pl)
                    c.te.wait(t_cp)
                    c.te.wait(t_yf[hp])
                    c.te.raw("matmul", ps_y[hp][:, :],
                             xt_sb[:, 64 * h:64 * (h + 1)], M16[hp][:, :],
                             start=True, stop=False)
                    c.te.raw("matmul", ps_y[hp][:, :],
                             S16[:, 64 * h:64 * (h + 1)], cpp[hp][:, :],
                             start=False, stop=True)
                    t_ymm = c.te.inc()
                    c.te.wait(t_dsf[h % 2])
                    c.te.raw("matmul", ps_ds[h % 2][:, :], btm[:, :],
                             xw16[hp][:, :], start=True, stop=True)
                    t_dsm = c.te.inc()
                    return t_ymm, t_dsm

                def back(h, toks):
                    global_ = None
                    hp = h % NP
                    t_ymm, t_dsm = toks
                    dd = c.dve
                    dd.wait(t_ymm)
                    dd.wait(t_ystg)
                    dd.raw("scalar_tensor_tensor",
                           ystg[64 * (h % 2):64 * (h % 2) + 64,
                                128 * (h // 2):128 * (h // 2 + 1)],
                           xbc[h // 2][64 * (h % 2):64 * (h % 2) + 64,
                                       lcol:lcol + 128],
                           Dbc[64 * (h % 2):64 * (h % 2) + 64, h:h + 1],
                           ps_y[hp][:, :], ALU.mult, ALU.add)
                    t_y = dd.inc()
                    t_yf[hp] = t_y
                    dd.wait(t_dsm)
                    dd.wait(t_gam)
                    dd.raw("scalar_tensor_tensor", S_all[:, 64 * h:64 * (h + 1)],
                           S_all[:, 64 * h:64 * (h + 1)], gam[:, h:h + 1],
                           ps_ds[h % 2][:, :], ALU.mult, ALU.add)
                    t_su = dd.inc()
                    t_dsf[h % 2] = t_su
                    c.act.wait(t_su)
                    c.act.raw("copy", S16[:, 64 * h:64 * (h + 1)],
                              S_all[:, 64 * h:64 * (h + 1)])
                    return c.act.inc()

                ft = {}
                mt = {}
                for hh in range(34):
                    if hh < 32:
                        ft[hh] = front(hh)
                    if 1 <= hh <= 32:
                        mt[hh - 1] = mid(hh - 1, ft.pop(hh - 1))
                    if hh >= 2:
                        t_s16 = back(hh - 2, mt.pop(hh - 2))
                t_row[pi] = t_yf[(31) % NP]
                last_y = t_yf[(31) % NP]
                c.sync.wait(ydma)
                for t_ in t_yf:
                    c.sync.wait(t_)
                ydma = c.qa.one(
                    y_d[:, :, 128 * cc:128 * (cc + 1)].rearrange("a p t -> p a t"),
                    ystg[:, :].rearrange("p (a t) -> p a t", t=128))
                t_ystg = ydma
                t_pre = t_yf[(31) % NP]
        c.barrier([ydma, t_s16] + [t for t in t_yf if t is not None])
    s_es.close()

    with ExitStack() as pes:
        outT = [c.sb(f"c_outT{k}", [128, 1024], BF16, es=pes) for k in range(16)]
        c.qw.begin()
        for k in range(16):
            c.qw.dma(outT[k][:, :], W["c_outT"][128 * k:128 * (k + 1), :])
        t_w = c.qw.end()
        yb = [c.sb(f"c_yb{k}", [128, CH], BF16, es=pes) for k in range(16)]
        zb = [c.sb(f"c_zb{k}", [128, CH], BF16, es=pes) for k in range(16)]
        zs = [c.sb(f"c_zs{i}", [128, CH], BF16, es=pes) for i in range(2)]
        yg = [c.sb(f"c_yg{k}", [128, CH], BF16, es=pes) for k in range(16)]
        ysq = [c.sb(f"c_ysq{i}", [128, CH], BF16, es=pes) for i in range(2)]
        onec = c.sb("c_onec", [128, 1], BF16, es=pes)
        rsy = c.sb("c_rsy", [1, CH], F32, es=pes)
        rsyc = c.sb("c_rsyc", [128, 4], F32, es=pes)
        t3r = [c.sb(f"c_t3r{i}", [128, 1024], F32, es=pes) for i in range(NB)]
        ub = [c.sb(f"c_ub{i}", [128, 1024], F32, es=pes) for i in range(NB)]
        ps = [pes.enter_context(nc.psum_tensor(f"c5_ps{i}", [128, 512], F32))
              for i in range(4)]
        ps_ss = pes.enter_context(nc.psum_tensor("c5_ss", [1, 512], F32))
        ps_rt = pes.enter_context(nc.psum_tensor("c5_rt", [128, 4], F32))
        c.dve.raw("memset", onec[:, :], 1.0)
        c.dve.drain()
        t_onec = c.dve.inc()
        c.te.wait(t_onec)
        c.te.wait(t_w)
        ld_tok = None
        yg_tok = None
        ss_tok = None
        rt_tok = None
        udma = [None] * NB
        bank_tok = [None] * 4
        for ci in range(NCHUNK):
            c.sync.wait(ld_tok)
            c.qa.begin()
            for k in range(16):
                c.qa.dma(yb[k][:, :], y_d[k, :, CH * ci:CH * (ci + 1)])
                c.qa.dma(zb[k][:, :], zxb_d[k, :, CH * ci:CH * (ci + 1)])
            t_ld = c.qa.end()
            mm_tok = [None, None]
            for k in range(16):
                si = k % 2
                c.act.wait(t_ld)
                c.act.wait(yg_tok)
                c.act.raw("activation", zs[si][:, :], zb[k][:, :], AF.Silu,
                          bias=0.0, scale=1.0)
                t_zs = c.act.inc()
                c.dve.wait(t_zs)
                c.dve.raw("tensor_tensor", yg[k][:, :], yb[k][:, :],
                          zs[si][:, :], ALU.mult)
                c.dve.drain()
                t_yg = c.dve.inc()
                c.act.wait(t_yg)
                c.act.wait(mm_tok[si])
                c.act.raw("activation", ysq[si][:, :], yg[k][:, :], AF.Square,
                          bias=0.0, scale=1.0)
                t_sq = c.act.inc()
                last_sq = t_sq
                c.te.wait(t_sq)
                c.te.wait(ss_tok)
                c.te.raw("matmul", ps_ss[:, :], onec[:, :], ysq[si][:, :],
                         start=(k == 0), stop=(k == 15))
                t_ssm = c.te.inc()
                mm_tok[si] = t_ssm
            yg_tok = t_ssm
            c.act.wait(t_ssm)
            c.act.wait(rt_tok)
            epst, epstok = c.const_col(1e-5)
            c.act.wait(epstok)
            c.act.raw("activation", rsy[0:1, :], ps_ss[0:1, :], AF.Sqrt,
                      bias=epst[0:1, 0:1], scale=1.0 / 2048.0)
            t_rs = c.act.inc()
            ss_tok = t_rs
            c.dve.wait(t_rs)
            c.dve.raw("reciprocal", rsy[0:1, :], rsy[0:1, :])
            c.dve.drain()
            t_rr = c.dve.inc()
            c.te.wait(t_rr)
            for ti in range(4):
                c.te.raw("transpose", ps_rt[:, ti:ti + 1],
                         rsy[0:1, 128 * ti:128 * (ti + 1)], c.identf[0:1, 0:1])
            t_rtm = c.te.inc()
            c.dve.wait(t_rtm)
            c.dve.raw("tensor_copy", rsyc[:, :], ps_rt[:, 0:4])
            c.dve.drain()
            t_rsc = c.dve.inc()
            rt_tok = t_rsc
            c.dve.wait(last_sq)
            for k in range(16):
                c.dve.raw("tensor_scalar", yg[k][:, :], yg[k][:, :],
                          mwc[:, k:k + 1], None, ALU.mult)
                c.dve.drain()
            t_ygm = c.dve.inc()
            for t_ in bank_tok:
                c.te.wait(t_)
            c.te.wait(t_ygm)
            for ti in range(4):
                i = 4 * ci + ti
                bi = i % NB
                c.dve.wait(udma[bi])
                tl = c.qb.one(t3r[bi][:, :], t3_d[128 * i:128 * (i + 1), :])
                for jc in range(2):
                    pb_ = (2 * ti + jc) % 4
                    c.te.wait(bank_tok[pb_])
                    for k in range(16):
                        c.te.raw("matmul", ps[pb_][:, :],
                                 yg[k][:, 128 * ti:128 * (ti + 1)],
                                 outT[k][:, 512 * jc:512 * (jc + 1)],
                                 start=(k == 0), stop=(k == 15))
                    t_mm = c.te.inc()
                    c.dve.wait(t_mm)
                    c.dve.wait(tl)
                    c.dve.wait(t_rsc)
                    c.dve.raw("scalar_tensor_tensor",
                              ub[bi][:, 512 * jc:512 * (jc + 1)], ps[pb_][:, :],
                              rsyc[:, ti:ti + 1],
                              t3r[bi][:, 512 * jc:512 * (jc + 1)],
                              ALU.mult, ALU.add)
                    c.dve.drain()
                    bank_tok[pb_] = c.dve.inc()
                t_u = bank_tok[(2 * ti + 1) % 4]
                c.sync.wait(t_u)
                udma[bi] = c.qb.one(u_d[128 * i:128 * (i + 1), :], ub[bi][:, :])
            ld_tok = t_u
        c.barrier([u for u in udma if u is not None] + [bank_tok[i] for i in range(4) if bank_tok[i] is not None])

    with ExitStack() as pes:
        m2T = [c.sb(f"c_m2T{k}", [128, 1024], BF16, es=pes) for k in range(32)]
        c.qw.begin()
        for k in range(32):
            c.qw.dma(m2T[k][:, :], W["c_m2T"][128 * k:128 * (k + 1), :])
        t_w2r = c.qw.end()
        m1b = [c.sb(f"c_m1b{s}", [128, 8 * 512], BF16, es=pes) for s in range(2)]
        m1w = [[m1b[s][:, 512 * k:512 * (k + 1)] for k in range(8)]
               for s in range(2)]
        ubr = [c.sb(f"c_ubr{i}", [128, 1024], F32, es=pes) for i in range(4)]
        xbr = [c.sb(f"c_xbr{i}", [128, 1024], F32, es=pes) for i in range(2)]
        hb = [c.sb(f"c_hb{i}", [128, 1024], BF16, es=pes) for i in range(2)]
        b3b = [c.sb(f"c_b3b{i}", [128, 1024], F32, es=pes) for i in range(4)]
        x3b = [c.sb(f"c_x3b{i}", [128, 1024], F32, es=pes) for i in range(2)]
        lns = [mk_ln_scratch(c, f"C6{i}", es=pes) for i in range(2)]
        lns2 = [mk_ln_scratch(c, f"C62{i}", es=pes) for i in range(2)]
        hn_fm = [c.sb(f"c_hnfm{k}", [128, CH], BF16, es=pes) for k in range(8)]
        hm = [c.sb(f"c_hm{k}", [128, CH], BF16, es=pes) for k in range(32)]
        ps_tr = pes.enter_context(nc.psum_tensor("c6_pstr", [128, 2048], BF16))
        ps = [pes.enter_context(nc.psum_tensor(f"c6_ps{i}", [128, 512], F32))
              for i in range(4)]
        c.te.wait(t_w2r)
        otoks = [None] * 2
        uld = [None] * 4
        xld = [None] * 2
        bank_tok = [None] * 4
        m1_tok = [None, None]
        half_tok = [None, None]
        entry_tok = [None] * 2
        prev_chunk = None
        b3_tok = [None] * 4
        for ci in range(NCHUNK):
            for ti in range(4):
                i = 4 * ci + ti
                bi = i % 2
                c.sync.wait(uld[ti])
                c.sync.wait(entry_tok[bi])
                tlu = c.qb.one(ubr[ti][:, :], u_d[128 * i:128 * (i + 1), :])
                c.dve.wait(tlu)
                _, rstd2, _ = ln_stats2(c, ubr[ti][:, :], lns[bi], eps=1e-6,
                                        scale=1024.0)
                c.dve.raw("scalar_tensor_tensor", hb[bi][:, :], ubr[ti][:, :],
                          rstd2, n2bc[:, :], ALU.mult, ALU.mult)
                c.dve.drain()
                t_hb = c.dve.inc()
                hf = i % 2
                c.te.wait(t_hb)
                c.te.wait(half_tok[hf])
                c.te.wait(prev_chunk)
                for k in range(8):
                    c.te.raw("transpose",
                             ps_tr[:, hf * 1024 + k * 128:hf * 1024 + (k + 1) * 128],
                             hb[bi][:, k * 128:(k + 1) * 128], c.ident16[:, :])
                t_tr = c.te.inc()
                c.dve.wait(t_tr)
                for k in range(8):
                    c.dve.raw("tensor_copy", hn_fm[k][:, 128 * ti:128 * (ti + 1)],
                              ps_tr[:, hf * 1024 + k * 128:hf * 1024 + (k + 1) * 128])
                c.dve.drain()
                entry_tok[bi] = half_tok[hf] = c.dve.inc()
            for t in entry_tok:
                c.te.wait(t)
            for q in range(8):
                s = q % 2
                c.pool.wait(m1_tok[s])
                c.qw.begin()
                for k in range(8):
                    c.qw.dma(m1w[s][k][:, :],
                             W["c_m1T"][128 * k:128 * (k + 1),
                                        512 * q:512 * (q + 1)])
                t_m1 = c.qw.end()
                c.te.wait(t_m1)
                for jj in range(4):
                    j = 4 * q + jj
                    pb_ = jj % 4
                    c.te.wait(bank_tok[pb_])
                    for k in range(8):
                        c.te.raw("matmul", ps[pb_][:, :],
                                 m1w[s][k][:, jj * 128:(jj + 1) * 128],
                                 hn_fm[k][:, :], start=(k == 0), stop=(k == 7))
                    t_mm = c.te.inc()
                    c.act.wait(t_mm)
                    c.act.wait(m1_tok[s])
                    c.act.raw("activation", hm[j][:, :], ps[pb_][:, :],
                              AF.Gelu, bias=mb1c[:, j:j + 1], scale=1.0)
                    bank_tok[pb_] = c.act.inc()
                m1_tok[s] = bank_tok[(4 * q + 3) % 4]
            for t_ in bank_tok:
                c.te.wait(t_)
            for ti in range(4):
                for jc in range(2):
                    pb_ = (2 * ti + jc) % 4
                    c.te.wait(bank_tok[pb_])
                    for k in range(32):
                        c.te.raw("matmul", ps[pb_][:, :],
                                 hm[k][:, 128 * ti:128 * (ti + 1)],
                                 m2T[k][:, 512 * jc:512 * (jc + 1)],
                                 start=(k == 0), stop=(k == 31))
                    t_mm = c.te.inc()
                    c.dve.wait(t_mm)
                    c.dve.wait(b3_tok[ti])
                    c.dve.raw("tensor_tensor",
                              b3b[ti][:, 512 * jc:512 * (jc + 1)], ps[pb_][:, :],
                              mb2bc[:, 512 * jc:512 * (jc + 1)], ALU.add)
                    c.dve.drain()
                    c.dve.raw("tensor_tensor",
                              b3b[ti][:, 512 * jc:512 * (jc + 1)],
                              b3b[ti][:, 512 * jc:512 * (jc + 1)],
                              ubr[ti][:, 512 * jc:512 * (jc + 1)], ALU.add)
                    c.dve.drain()
                    bank_tok[pb_] = c.dve.inc()
            prev_chunk = bank_tok[(2 * 3 + 1) % 4]
            for ti in range(4):
                i = 4 * ci + ti
                bi = i % 2
                c.dve.wait(otoks[bi])
                c.dve.wait(xld[bi])
                tlx = c.qb.one(xbr[bi][:, :], x_in[128 * i:128 * (i + 1), :])
                mean, rstd, mrs = ln_stats(c, b3b[ti][:, :], lns2[bi])
                c.dve.wait(tlx)
                c.dve.raw("tensor_scalar", x3b[bi][:, :], xbr[bi][:, :], mrs,
                          None, ALU.subtract)
                c.dve.drain()
                c.dve.raw("scalar_tensor_tensor", x3b[bi][:, :], b3b[ti][:, :],
                          rstd, x3b[bi][:, :], ALU.mult, ALU.add)
                c.dve.drain()
                t_x3 = c.dve.inc()
                b3_tok[ti] = t_x3
                uld[ti] = t_x3
                xld[bi] = t_x3
                c.sync.wait(t_x3)
                otoks[bi] = c.qb.one(x_out[128 * i:128 * (i + 1), :],
                                     x3b[bi][:, :])
    return [t for t in otoks if t is not None]



def build_program(stages=("A",)):
    nc = bass.Bass("TRN2", target_bir_lowering=False, debug=False)
    es = ExitStack()
    x0 = nc.dram_tensor("x0", [T, D], F32, kind="ExternalInput").ap()
    out = nc.dram_tensor("out", [T, D], F32, kind="ExternalOutput").ap()
    W = {}

    def win(name, shape):
        W[name] = nc.dram_tensor(name, list(shape), F32, kind="ExternalInput").ap()

    with es:
        c = Ctx(nc, es)
        build_ident(c)
        cur_in = x0
        final = None
        nstage = len(stages)
        sidx = 0
        xs = [nc.dram_tensor(f"xs{i}", [T, D], F32, kind="Internal").ap()
              for i in range(nstage - 1)]

        def nxt():
            nonlocal sidx
            o = out if sidx == nstage - 1 else xs[sidx]
            sidx += 1
            return o

        W16 = {}
        W16D = {}
        if "D" in stages:
            win("d_nw", [1024]); win("d_gateT", [1024, 8])
            win("d_w1T", [8 * 1024, 2048]); win("d_b1", [8, 2048])
            win("d_w2T", [8 * 2048, 1024]); win("d_b2", [8, 1024])
            W16D["w1T"] = nc.dram_tensor("d16_w1T", [8 * 1024, 2048], BF16,
                                         kind="Internal").ap()
            W16D["w2T"] = nc.dram_tensor("d16_w2T", [8 * 2048, 1024], BF16,
                                         kind="Internal").ap()
            stg_toks_d = stage_weights_bf16(c, [
                (W["d_w1T"], W16D["w1T"], 8 * 1024, 2048),
                (W["d_w2T"], W16D["w2T"], 8 * 2048, 1024),
            ])
        if "B" in stages:
            win("bw1T", [1024, 4096]); win("bw2T", [1024, 4096])
            win("bh1T", [4096, 4096]); win("bh2T", [4096, 4096])
            win("bw3T", [4096, 1024])
            win("w1_s", [4096]); win("w2_s", [4096])
            win("h1_s", [4096]); win("h2_s", [4096]); win("w3_s", [1024])
            for nm, shp in (("w1T", [1024, 4096]), ("w2T", [1024, 4096]),
                            ("h1T", [4096, 4096]), ("h2T", [4096, 4096]),
                            ("w3T", [4096, 1024])):
                W16[nm] = nc.dram_tensor(f"b16_{nm}", shp, BF16,
                                         kind="Internal").ap()
            stg_toks = stage_weights_bf16(c, [
                (W["bw1T"], W16["w1T"], 1024, 4096),
                (W["bw2T"], W16["w2T"], 1024, 4096),
                (W["bh1T"], W16["h1T"], 4096, 4096),
                (W["bh2T"], W16["h2T"], 4096, 4096),
                (W["bw3T"], W16["w3T"], 4096, 1024),
            ])
        if "A" in stages:
            win("w_vT", [1024, 1024]); win("b_v", [1024])
            win("w_outT", [1024, 1024]); win("b_out", [1024])
            o = nxt()
            with ExitStack() as ses:
                final = stage_A(c, cur_in, o, W, ses)
                c.barrier(final)
                c.drain_queues()
            cur_in = o
        if "B" in stages:
            c.barrier(stg_toks)
            o = nxt()
            with ExitStack() as ses:
                final = stage_B(c, cur_in, o, W, W16, ses)
                c.barrier(final)
                c.drain_queues()
            cur_in = o
        if "C" in stages:
            win("c_n1w", [1024]); win("c_inpT", [1024, 4384])
            win("c_convw", [2304, 4]); win("c_convb", [2304])
            win("c_dtb", [32]); win("c_Alog", [32]); win("c_D", [32])
            win("c_mnw", [2048]); win("c_outT", [2048, 1024])
            win("c_n2w", [1024]); win("c_m1T", [1024, 4096])
            win("c_mb1", [4096]); win("c_m2T", [4096, 1024])
            win("c_mb2", [1024])
            o = nxt()
            with ExitStack() as ses:
                final = stage_C(c, cur_in, o, W, ses)
                c.barrier(final)
                c.drain_queues()
            cur_in = o
        if "D" in stages:
            c.barrier(stg_toks_d)
            o = nxt()
            with ExitStack() as ses:
                final = stage_D(c, cur_in, o, W, W16D, ses)
                c.barrier(final)
                c.drain_queues()
            cur_in = o
        for t in final:
            c.sync.wait(t)
        c.flush()
    return nc, list(W.keys())


def prep_weights(inputs, stages=("A",)):
    t = lambda a: np.ascontiguousarray(np.asarray(a).T)
    r = lambda a: np.ascontiguousarray(np.asarray(a))
    prep = {}
    if "A" in stages:
        v_rows = np.arange(3072).reshape(16, 3, 64)[:, 2, :].reshape(-1)
        prep["w_vT"] = np.ascontiguousarray(np.asarray(inputs["L0_attn_w_in"])[v_rows].T)
        prep["b_v"] = r(np.asarray(inputs["L0_attn_b_in"])[v_rows])
        prep["w_outT"] = t(inputs["L0_attn_w_out"])
        prep["b_out"] = r(inputs["L0_attn_b_out"])
    if "B" in stages:
        prep["bw1T"] = t(inputs["L0_ffn_w1_w"])
        prep["bw2T"] = t(inputs["L0_ffn_w2_w"])
        prep["bh1T"] = t(inputs["L0_ffn_h1_w"])
        prep["bh2T"] = t(inputs["L0_ffn_h2_w"])
        prep["bw3T"] = t(inputs["L0_ffn_w3_w"])
        prep["w1_s"] = r(inputs["L0_ffn_w1_s"])
        prep["w2_s"] = r(inputs["L0_ffn_w2_s"])
        prep["h1_s"] = r(inputs["L0_ffn_h1_s"])
        prep["h2_s"] = r(inputs["L0_ffn_h2_s"])
        prep["w3_s"] = r(inputs["L0_ffn_w3_s"])
    if "C" in stages:
        prep["c_n1w"] = r(inputs["L1_attn_norm1_w"])
        prep["c_inpT"] = t(inputs["L1_attn_in_proj_w"])
        prep["c_convw"] = r(inputs["L1_attn_conv_w"])
        prep["c_convb"] = r(inputs["L1_attn_conv_b"])
        prep["c_dtb"] = r(inputs["L1_attn_dt_bias"])
        prep["c_Alog"] = r(inputs["L1_attn_A_log"])
        prep["c_D"] = r(inputs["L1_attn_D"])
        prep["c_mnw"] = r(inputs["L1_attn_mnorm_w"])
        prep["c_outT"] = t(inputs["L1_attn_out_proj_w"])
        prep["c_n2w"] = r(inputs["L1_attn_norm2_w"])
        prep["c_m1T"] = t(inputs["L1_attn_mlp_w1"])
        prep["c_mb1"] = r(inputs["L1_attn_mlp_b1"])
        prep["c_m2T"] = t(inputs["L1_attn_mlp_w2"])
        prep["c_mb2"] = r(inputs["L1_attn_mlp_b2"])
    if "D" in stages:
        prep["d_nw"] = r(inputs["L1_ffn_norm_weight"])
        prep["d_gateT"] = t(inputs["L1_ffn_gate_w"])
        prep["d_w1T"] = np.ascontiguousarray(
            np.asarray(inputs["L1_ffn_w1"]).transpose(0, 2, 1).reshape(8 * 1024, 2048))
        prep["d_b1"] = r(inputs["L1_ffn_b1"])
        prep["d_w2T"] = np.ascontiguousarray(
            np.asarray(inputs["L1_ffn_w2"]).transpose(0, 2, 1).reshape(8 * 2048, 1024))
        prep["d_b2"] = r(inputs["L1_ffn_b2"])
    return prep


def run_stages(x_full, inputs, stages=("A",)):
    x = np.ascontiguousarray(np.asarray(x_full).reshape(16, 2048, 1024))
    shards = [np.ascontiguousarray(x[2 * i:2 * i + 2].reshape(T, D))
              for i in range(NCORES)]
    nc, wnames = build_program(stages=stages)
    prep = prep_weights(inputs, stages)
    in_maps = [dict({"x0": shards[i]}, **{n: prep[n] for n in wnames})
               for i in range(NCORES)]
    res = run_bass_kernel_spmd(nc, in_maps, core_ids=list(range(NCORES)))
    return np.concatenate(
        [r["out"].reshape(2, 2048, 1024) for r in res.results], axis=0)


def kernel(**inputs):
    return run_stages(inputs["L0_attn_x"], inputs,
                      stages=("A", "B", "C", "D"))


# version 17
# speedup vs baseline: 1.0135x; 1.0135x over previous
import numpy as np
import ml_dtypes
import concourse.bass as bass
import concourse.mybir as mybir
from concourse.bass_utils import run_bass_kernel_spmd
from contextlib import ExitStack

F32 = mybir.dt.float32
BF16 = mybir.dt.bfloat16
AF = mybir.ActivationFunctionType
ALU = mybir.AluOpType
AX = mybir.AxisListType

B_LOC = 2
L = 2048
T = B_LOC * L
D = 1024
NCORES = 8
NT = T // 128



class Tok:
    __slots__ = ("sem", "val")

    def __init__(self, sem, val):
        self.sem, self.val = sem, val


class Em:

    def __init__(self, ctx, ename, sems):
        self.ctx, self.ename = ctx, ename
        self.sems = sems
        self.cnt = [0] * len(sems)
        self.rr = 0
        self.ops = []
        self.waited = {}

    def raw(self, meth, *args, **kwargs):
        self.ops.append([meth, args, kwargs, []])
        return self

    def inc(self, rotate=False):
        if rotate:
            self.rr = (self.rr + 1) % len(self.sems)
        i = self.rr
        self.cnt[i] += 1
        assert self.ops and self.ops[-1][0] != "WAIT"
        self.ops[-1][3].append((self.sems[i], 1))
        return Tok(self.sems[i], self.cnt[i])

    def tok(self):
        return Tok(self.sems[self.rr], self.cnt[self.rr])

    def wait(self, tok):
        if tok is None:
            return
        k = id(tok.sem)
        if self.waited.get(k, -1) >= tok.val:
            return
        self.waited[k] = tok.val
        self.ops.append(["WAIT", tok.sem, tok.val, None])

    def flush(self, eng):
        for op in self.ops:
            if op[0] == "WAIT":
                eng.wait_ge(op[1], op[2])
            else:
                inst = getattr(eng, op[0])(*op[1], **op[2])
                for sem, n in op[3]:
                    inst.then_inc(sem, n)

    def drain(self):
        return self.raw("drain")


class DmaQ:

    def __init__(self, ctx, em, nsems, name):
        self.em = em
        self.sems = [ctx.sem(f"{name}{i}") for i in range(nsems)]
        self.cnt = [0] * nsems
        self.rr = -1
        self.cur = None

    def begin(self):
        self.rr = (self.rr + 1) % len(self.sems)
        self.cur = self.rr
        return self

    def dma(self, out, in_, **kw):
        assert self.cur is not None
        self.em.raw("dma_start", out=out, in_=in_, **kw)
        self.em.ops[-1][3].append((self.sems[self.cur], 16))
        self.cnt[self.cur] += 1
        return self

    def end(self):
        t = Tok(self.sems[self.cur], 16 * self.cnt[self.cur])
        self.cur = None
        return t

    def one(self, out, in_, **kw):
        self.begin()
        self.dma(out, in_, **kw)
        return self.end()


class Ctx:
    def __init__(self, nc, es):
        self.nc, self.es = nc, es
        self.te = Em(self, "tensor", [self.sem("s_te")])
        self.dve = Em(self, "vector", [self.sem("s_dve")])
        self.act = Em(self, "scalar", [self.sem("s_act")])
        self.pool = Em(self, "gpsimd", [self.sem(f"s_pl{i}") for i in range(4)])
        self.sync = Em(self, "sync", [self.sem("s_sy")])
        self.qa = DmaQ(self, self.sync, 8, "qa")
        self.qb = DmaQ(self, self.sync, 8, "qb")
        self.qw = DmaQ(self, self.pool, 8, "qw")
        self.qp = DmaQ(self, self.pool, 8, "qp")

    def sem(self, name):
        return self.es.enter_context(self.nc.semaphore(name))

    def sb(self, name, shape, dtype, es=None):
        return (es or self.es).enter_context(self.nc.sbuf_tensor(name, shape, dtype))

    def barrier(self, toks):
        for t in (toks if isinstance(toks, (list, tuple)) else [toks]):
            for em in (self.te, self.dve, self.act, self.pool, self.sync):
                em.wait(t)

    def drain_queues(self):
        toks = []
        for q in (self.qa, self.qb, self.qw):
            toks += [Tok(s, 16 * n) for s, n in zip(q.sems, q.cnt) if n]
        self.barrier(toks)

    def const_col(self, val):
        if not hasattr(self, "_cc"):
            self._cc = {}
        if val not in self._cc:
            t = self.sb(f"cc{len(self._cc)}", [128, 1], F32)
            self.dve.raw("memset", t[:, :], float(val))
            self.dve.drain()
            tok = self.dve.inc()
            self._cc[val] = (t, tok)
        return self._cc[val]

    def flush(self):
        with self.nc.Block() as blk:
            blk.tensor(lambda e: self.te.flush(e))
            blk.vector(lambda e: self.dve.flush(e))
            blk.scalar(lambda e: self.act.flush(e))
            blk.gpsimd(lambda e: self.pool.flush(e))
            blk.sync(lambda e: self.sync.flush(e))



def build_ident(c):
    c.ident16 = c.sb("ident16", [128, 128], BF16)
    c.identf = c.sb("identf", [128, 128], F32)
    iop = c.iotap = c.sb("iotap", [128, 1], F32)
    iof = c.iotaf = c.sb("iotaf", [128, 128], F32)
    c.pool.raw("iota", iop[:, :], [[0, 1]], channel_multiplier=1,
               allow_small_or_imprecise_dtypes=True)
    c.pool.raw("iota", iof[:, :], [[1, 128]], channel_multiplier=0,
               allow_small_or_imprecise_dtypes=True)
    tk = c.pool.inc()
    c.dve.wait(tk)
    c.dve.raw("tensor_scalar", c.identf[:, :], iof[:, :], iop[:, 0:1], None,
              ALU.is_equal)
    c.dve.drain()
    c.dve.raw("tensor_copy", c.ident16[:, :], c.identf[:, :])
    c.dve.drain()
    c.ident_tok = c.dve.inc()
    c.te.wait(c.ident_tok)
    c.const_col(1e-5)
    c.const_col(1e-6)
    c.const_col(1.0)


def mk_ln_scratch(c, tag, es=None):
    return {
        "stats": c.sb(f"lns_st{tag}", [128, 12], F32, es=es),
        "aggr": c.sb(f"lns_ag{tag}", [128, 2], F32, es=es),
        "srt": c.sb(f"lns_sr{tag}", [128, 1], F32, es=es),
        "rstd": c.sb(f"lns_rs{tag}", [128, 1], F32, es=es),
        "mrs": c.sb(f"lns_mr{tag}", [128, 1], F32, es=es),
        "t_act": None, "t_dve": None,
    }


def ln_stats(c, x_ap, sc, eps=1e-5):
    c.dve.wait(sc["t_act"])
    F = x_ap.shape[-1]
    ng = (F + 511) // 512
    for g in range(ng):
        c.dve.raw("bn_stats", sc["stats"][:, 6 * g:6 * (g + 1)],
                  x_ap[:, 512 * g:min(512 * (g + 1), F)])
    c.dve.drain()
    c.dve.raw("bn_aggr", sc["aggr"][:, :], sc["stats"][:, 0:6 * ng])
    t_ag = c.dve.inc()
    c.act.wait(t_ag)
    c.act.wait(sc["t_dve"])
    epst, epstok = c.const_col(eps)
    c.act.wait(epstok)
    c.act.raw("activation", sc["srt"][:, 0:1], sc["aggr"][:, 1:2], AF.Sqrt,
              bias=epst[:, 0:1], scale=1.0)
    sc["t_act"] = c.act.inc()
    c.dve.wait(sc["t_act"])
    c.dve.raw("reciprocal", sc["rstd"][:, 0:1], sc["srt"][:, 0:1])
    c.dve.drain()
    c.dve.raw("tensor_tensor", sc["mrs"][:, 0:1], sc["aggr"][:, 0:1],
              sc["rstd"][:, 0:1], ALU.mult)
    c.dve.drain()
    sc["t_dve"] = c.dve.inc()
    return sc["aggr"][:, 0:1], sc["rstd"][:, 0:1], sc["mrs"][:, 0:1]


def ln_stats2(c, x_ap, sc, eps, scale=1.0):
    c.dve.wait(sc["t_act"])
    F = x_ap.shape[-1]
    ng = (F + 511) // 512
    for g in range(ng):
        c.dve.raw("bn_stats", sc["stats"][:, 6 * g:6 * (g + 1)],
                  x_ap[:, 512 * g:min(512 * (g + 1), F)])
    c.dve.drain()
    c.dve.raw("bn_aggr", sc["aggr"][:, :], sc["stats"][:, 0:6 * ng])
    c.dve.drain()
    c.dve.raw("tensor_tensor", sc["mrs"][:, 0:1], sc["aggr"][:, 0:1],
              sc["aggr"][:, 0:1], ALU.mult)
    c.dve.drain()
    c.dve.raw("tensor_tensor", sc["mrs"][:, 0:1], sc["mrs"][:, 0:1],
              sc["aggr"][:, 1:2], ALU.add)
    t_ms = c.dve.inc()
    c.act.wait(t_ms)
    c.act.wait(sc["t_dve"])
    epst, epstok = c.const_col(eps)
    c.act.wait(epstok)
    c.act.raw("activation", sc["srt"][:, 0:1], sc["mrs"][:, 0:1], AF.Sqrt,
              bias=epst[:, 0:1], scale=float(scale))
    sc["t_act"] = c.act.inc()
    c.dve.wait(sc["t_act"])
    c.dve.raw("reciprocal", sc["rstd"][:, 0:1], sc["srt"][:, 0:1])
    c.dve.drain()
    sc["t_dve"] = c.dve.inc()
    return None, sc["rstd"][:, 0:1], None



def stage_A(c, x_in, x_out, W, ses):
    nc = c.nc
    w_vT = [c.sb(f"a_wvT{k}", [128, 1024], BF16, es=ses) for k in range(8)]
    w_outT = [c.sb(f"a_woutT{k}", [128, 1024], BF16, es=ses) for k in range(8)]
    bvcols = c.sb("a_bvcols", [128, 8], F32, es=ses)
    bout_bc = c.sb("a_boutbc", [128, 1024], F32, es=ses)
    v_fm = [c.sb(f"a_vfm{k}", [128, T], BF16, es=ses) for k in range(8)]

    c.qw.begin()
    for k in range(8):
        c.qw.dma(w_vT[k][:, :], W["w_vT"][128 * k:128 * (k + 1), :])
    for k in range(8):
        c.qw.dma(w_outT[k][:, :], W["w_outT"][128 * k:128 * (k + 1), :])
    t_w = c.qw.end()
    c.qa.begin()
    for j in range(8):
        c.qa.dma(bvcols[:, j:j + 1],
                 W["b_v"][128 * j:128 * (j + 1)].rearrange("(p o) -> p o", o=1))
    c.qa.dma(bout_bc[:, :],
             W["b_out"].rearrange("(o f) -> o f", o=1).partition_broadcast(128))
    t_bc = c.qa.end()

    NB = 3
    xb = [c.sb(f"a_xb{i}", [128, 1024], F32, es=ses) for i in range(NB)]
    t1b = [c.sb(f"a_t1b{i}", [128, 1024], BF16, es=ses) for i in range(NB)]
    lns = [mk_ln_scratch(c, f"A{i}", es=ses) for i in range(NB)]
    t1_es = ExitStack()
    t1_fm = [c.sb(f"a_t1fm{k}", [128, T], BF16, es=t1_es) for k in range(8)]

    with nc.psum_tensor("a_pstr", [128, 2048], BF16) as ps_tr:
        btok = [None] * NB
        half_tok = [None, None]
        for i in range(NT):
            bi = i % NB
            c.sync.wait(btok[bi])
            tl = c.qb.one(xb[bi][:, :], x_in[128 * i:128 * (i + 1), :])
            c.dve.wait(tl)
            mean, rstd, mrs = ln_stats(c, xb[bi][:, :], lns[bi])
            c.dve.raw("tensor_scalar", t1b[bi][:, :], xb[bi][:, :],
                      mean, rstd, ALU.subtract, ALU.mult)
            c.dve.drain()
            t_t1 = c.dve.inc()
            hf = i % 2
            c.te.wait(half_tok[hf])
            c.te.wait(t_t1)
            for k in range(8):
                c.te.raw("transpose",
                         ps_tr[:, hf * 1024 + k * 128:hf * 1024 + (k + 1) * 128],
                         t1b[bi][:, k * 128:(k + 1) * 128], c.ident16[:, :])
            t_tr = c.te.inc()
            c.dve.wait(t_tr)
            for k in range(8):
                c.dve.raw("tensor_copy", t1_fm[k][:, 128 * i:128 * (i + 1)],
                          ps_tr[:, hf * 1024 + k * 128:hf * 1024 + (k + 1) * 128])
            btok[bi] = half_tok[hf] = c.dve.inc()
        for t in btok:
            c.te.wait(t)

    with ExitStack() as ps_es:
        ps_q = [ps_es.enter_context(nc.psum_tensor(f"a_psq{i}", [128, 512], F32))
                for i in range(4)]
        c.te.wait(t_w)
        bank_tok = [None] * 4
        for j in range(8):
            for n in range(T // 512):
                pb = n % 4
                c.te.wait(bank_tok[pb])
                for k in range(8):
                    c.te.raw("matmul", ps_q[pb][:, :],
                             w_vT[k][:, j * 128:(j + 1) * 128],
                             t1_fm[k][:, n * 512:(n + 1) * 512],
                             start=(k == 0), stop=(k == 7))
                t_mm = c.te.inc()
                c.dve.wait(t_mm)
                c.dve.wait(t_bc)
                c.dve.raw("tensor_scalar", v_fm[j][:, n * 512:(n + 1) * 512],
                          ps_q[pb][:, :], bvcols[:, j:j + 1], None, ALU.add)
                bank_tok[pb] = c.dve.inc()
        t_v = bank_tok[(T // 512 - 1) % 4]
        for t in bank_tok:
            c.te.wait(t)
    t1_es.close()
    c.barrier([t for t in bank_tok if t is not None])

    with ExitStack() as ps_es:
        ps_w = [ps_es.enter_context(nc.psum_tensor(f"a_psw{i}", [128, 512], F32))
                for i in range(4)]
        a1b = [c.sb(f"a_a1b{i}", [128, 1024], F32, es=ps_es) for i in range(NB)]
        x1b = [c.sb(f"a_x1b{i}", [128, 1024], F32, es=ps_es) for i in range(NB)]
        lns2 = [mk_ln_scratch(c, f"A2{i}", es=ps_es) for i in range(NB)]
        otoks = [None] * NB
        bank_tok = [None] * 4
        for i in range(NT):
            bi = i % NB
            c.sync.wait(otoks[bi])
            tl = c.qb.one(xb[bi][:, :], x_in[128 * i:128 * (i + 1), :])
            for jc in range(2):
                pb = (2 * i + jc) % 4
                c.te.wait(bank_tok[pb])
                for k in range(8):
                    c.te.raw("matmul", ps_w[pb][:, :],
                             v_fm[k][:, 128 * i:128 * (i + 1)],
                             w_outT[k][:, jc * 512:(jc + 1) * 512],
                             start=(k == 0), stop=(k == 7))
                t_mm = c.te.inc()
                c.dve.wait(t_mm)
                c.dve.raw("tensor_tensor", a1b[bi][:, jc * 512:(jc + 1) * 512],
                          ps_w[pb][:, :], bout_bc[:, jc * 512:(jc + 1) * 512],
                          ALU.add)
                bank_tok[pb] = c.dve.inc()
            c.dve.drain()
            mean, rstd, mrs = ln_stats(c, a1b[bi][:, :], lns2[bi])
            c.dve.wait(tl)
            c.dve.raw("tensor_scalar", x1b[bi][:, :], xb[bi][:, :], mrs, None,
                      ALU.subtract)
            c.dve.drain()
            c.dve.raw("scalar_tensor_tensor", x1b[bi][:, :], a1b[bi][:, :],
                      rstd, x1b[bi][:, :], ALU.mult, ALU.add)
            c.dve.drain()
            t_x1 = c.dve.inc()
            c.sync.wait(t_x1)
            otoks[bi] = c.qb.one(x_out[128 * i:128 * (i + 1), :], x1b[bi][:, :])
    return [t for t in otoks if t is not None]




def stage_weights_bf16(c, specs):
    toks = {}
    for src_ap, dst_ap, nrows, ncols in specs:
        for r0 in range(0, nrows, 2048):
            rw = min(2048, nrows - r0)
            t = c.qw.one(dst_ap[r0:r0 + rw, :], src_ap[r0:r0 + rw, :])
            toks[id(t.sem)] = t
    return list(toks.values())


def load_cols(c, q, dst, src_ap, n, tag=None):
    q.begin()
    for j in range(n):
        q.dma(dst[:, j:j + 1],
              src_ap[128 * j:128 * (j + 1)].rearrange("(p o) -> p o", o=1))
    return q.end()


def entry_tile(c, x_row_ap, xb, tb, lns, fm_tiles, col0, ps_tr, hf, eps=1e-5):
    tl = c.qb.one(xb[:, :], x_row_ap)
    c.dve.wait(tl)
    mean, rstd, mrs = ln_stats(c, xb[:, :], lns, eps=eps)
    c.dve.raw("tensor_scalar", tb[:, :], xb[:, :], mean, rstd,
              ALU.subtract, ALU.mult)
    c.dve.drain()
    t_t1 = c.dve.inc()
    c.te.wait(t_t1)
    nk = tb.shape[1] // 128
    for k in range(nk):
        c.te.raw("transpose",
                 ps_tr[:, hf * 1024 + k * 128:hf * 1024 + (k + 1) * 128],
                 tb[:, k * 128:(k + 1) * 128], c.ident16[:, :])
    t_tr = c.te.inc()
    c.dve.wait(t_tr)
    for k in range(nk):
        c.dve.raw("tensor_copy", fm_tiles[k][:, col0:col0 + 128],
                  ps_tr[:, hf * 1024 + k * 128:hf * 1024 + (k + 1) * 128])
    return c.dve.inc()


def stage_B(c, x_in, x_out, W, W16, ses):
    nc = c.nc
    CH = 512
    NCHUNK = T // CH
    scols = {}
    for nm in ("w1_s", "w2_s", "h1_s", "h2_s"):
        scols[nm] = c.sb(f"b_{nm}c", [128, 32], F32, es=ses)
        load_cols(c, c.qa, scols[nm], W[nm], 32)
    s3bc = c.sb("b_s3bc", [128, 1024], F32, es=ses)
    t_s3 = c.qa.one(s3bc[:, :],
                    W["w3_s"].rearrange("(o f) -> o f", o=1).partition_broadcast(128))
    col_toks = [Tok(s, 16 * n) for s, n in zip(c.qa.sems, c.qa.cnt) if n]
    c.barrier(col_toks + [t_s3])

    NB = 2
    xb = [c.sb(f"b_xb{i}", [128, 1024], F32, es=ses) for i in range(NB)]
    tb = [c.sb(f"b_tb{i}", [128, 1024], BF16, es=ses) for i in range(NB)]
    lns = [mk_ln_scratch(c, f"B{i}", es=ses) for i in range(NB)]
    lns2 = [mk_ln_scratch(c, f"B2{i}", es=ses) for i in range(NB)]
    a1b = [c.sb(f"b_a1b{i}", [128, 1024], F32, es=ses) for i in range(4)]
    x1b = [c.sb(f"b_x1b{i}", [128, 1024], F32, es=ses) for i in range(NB)]
    u1sb = [c.sb(f"b_u1{i}", [128, CH], BF16, es=ses) for i in range(2)]
    hA = [c.sb(f"b_hA{i}", [128, CH], BF16, es=ses) for i in range(32)]
    hB = [c.sb(f"b_hB{i}", [128, CH], BF16, es=ses) for i in range(32)]
    t2_fm = [c.sb(f"b_t2fm{k}", [128, CH], BF16, es=ses) for k in range(8)]
    wbuf = [c.sb(f"b_wb{s}", [128, 32 * 512], BF16, es=ses) for s in range(2)]
    wq = [[wbuf[s][:, 512 * k:512 * (k + 1)] for k in range(32)]
          for s in range(2)]
    wv = [[wbuf[s][:, 1024 * k:1024 * (k + 1)] for k in range(8)]
          for s in range(2)]

    otoks = [None] * NB
    entry_tok = [None] * NB
    prev_chunk_tok = None
    wv_tok = [None, None]
    wq_tok = [None, None]
    with ExitStack() as ps_es:
        ps_tr = ps_es.enter_context(nc.psum_tensor("b_pstr", [128, 2048], BF16))
        ps = [ps_es.enter_context(nc.psum_tensor(f"b_ps{i}", [128, 512], F32))
              for i in range(4)]
        bank_tok = [None] * 4
        for ci in range(NCHUNK):
            for ti in range(4):
                i = 4 * ci + ti
                bi = i % NB
                c.sync.wait(entry_tok[bi])
                c.sync.wait(otoks[bi])
                hf = i % 2
                if prev_chunk_tok is not None:
                    c.te.wait(prev_chunk_tok)
                entry_tok[bi] = entry_tile(
                    c, x_in[128 * i:128 * (i + 1), :], xb[bi], tb[bi],
                    lns[bi], t2_fm, 128 * ti, ps_tr, hf)
            t_entry = entry_tok[(4 * ci + 3) % NB]
            for t in entry_tok:
                c.te.wait(t)
            for q in range(8):
                s = q % 2
                c.sync.wait(wv_tok[s])
                c.qa.begin()
                for k in range(8):
                    c.qa.dma(wv[s][k][:, 0:512],
                             W16["w1T"][128 * k:128 * (k + 1),
                                        512 * q:512 * (q + 1)])
                    c.qa.dma(wv[s][k][:, 512:1024],
                             W16["w2T"][128 * k:128 * (k + 1),
                                        512 * q:512 * (q + 1)])
                t_wvq = c.qa.end()
                c.te.wait(t_wvq)
                for jj in range(4):
                    j = 4 * q + jj
                    pa, pb_ = 2 * (jj % 2), 2 * (jj % 2) + 1
                    c.te.wait(bank_tok[pa])
                    for k in range(8):
                        c.te.raw("matmul", ps[pa][:, :],
                                 wv[s][k][:, jj * 128:(jj + 1) * 128],
                                 t2_fm[k][:, :], start=(k == 0), stop=(k == 7))
                    t_mma = c.te.inc()
                    c.te.wait(bank_tok[pb_])
                    for k in range(8):
                        c.te.raw("matmul", ps[pb_][:, :],
                                 wv[s][k][:, 512 + jj * 128:512 + (jj + 1) * 128],
                                 t2_fm[k][:, :], start=(k == 0), stop=(k == 7))
                    t_mmb = c.te.inc()
                    ui = jj % 2
                    c.act.wait(t_mma)
                    c.act.wait(bank_tok[pa])
                    c.act.raw("activation", u1sb[ui][:, :], ps[pa][:, :],
                              AF.Silu, bias=0.0, scale=scols["w1_s"][:, j:j + 1])
                    t_u1 = c.act.inc()
                    c.dve.wait(t_u1)
                    c.dve.wait(t_mmb)
                    c.dve.raw("scalar_tensor_tensor", hA[j][:, :], ps[pb_][:, :],
                              scols["w2_s"][:, j:j + 1], u1sb[ui][:, :],
                              ALU.mult, ALU.mult)
                    bank_tok[pa] = bank_tok[pb_] = c.dve.inc()
                wv_tok[s] = bank_tok[pa]
            for t_ in bank_tok:
                c.te.wait(t_)
            wq_tok[0] = wq_tok[1] = bank_tok[pa]
            for phase, (src_t, dst_t, sc_nm, w16nm) in enumerate(
                    ((hA, hB, "h1_s", "h1T"), (hB, hA, "h2_s", "h2T"))):
                for q in range(8):
                    s = q % 2
                    c.sync.wait(wq_tok[s])
                    c.qa.begin()
                    for k in range(32):
                        c.qa.dma(wq[s][k][:, :],
                                 W16[w16nm][128 * k:128 * (k + 1),
                                            512 * q:512 * (q + 1)])
                    t_wqq = c.qa.end()
                    c.te.wait(t_wqq)
                    for jj in range(4):
                        j = 4 * q + jj
                        pb_ = jj % 4
                        c.te.wait(bank_tok[pb_])
                        for k in range(32):
                            c.te.raw("matmul", ps[pb_][:, :],
                                     wq[s][k][:, jj * 128:(jj + 1) * 128],
                                     src_t[k][:, :], start=(k == 0), stop=(k == 31))
                        t_mm = c.te.inc()
                        c.act.wait(t_mm)
                        c.act.raw("activation", dst_t[j][:, :], ps[pb_][:, :],
                                  AF.Gelu, bias=0.0,
                                  scale=scols[sc_nm][:, j:j + 1])
                        bank_tok[pb_] = c.act.inc()
                    wq_tok[s] = bank_tok[(4 * q + 3) % 4]
                for t_ in bank_tok:
                    c.te.wait(t_)
            a1_tok = None
            for jc in range(2):
                s = jc
                c.sync.wait(wq_tok[s])
                c.qa.begin()
                for k in range(32):
                    c.qa.dma(wq[s][k][:, :],
                             W16["w3T"][128 * k:128 * (k + 1),
                                        512 * jc:512 * (jc + 1)])
                t_w3q = c.qa.end()
                c.te.wait(t_w3q)
                for ti in range(4):
                    pb_ = ti % 4
                    c.te.wait(bank_tok[pb_])
                    for k in range(32):
                        c.te.raw("matmul", ps[pb_][:, :],
                                 hA[k][:, ti * 128:(ti + 1) * 128],
                                 wq[s][k][:, :], start=(k == 0), stop=(k == 31))
                    t_mm = c.te.inc()
                    c.dve.wait(t_mm)
                    c.dve.wait(t_s3)
                    c.dve.raw("tensor_tensor",
                              a1b[ti][:, jc * 512:(jc + 1) * 512],
                              ps[pb_][:, :], s3bc[:, jc * 512:(jc + 1) * 512],
                              ALU.mult)
                    bank_tok[pb_] = c.dve.inc()
                wq_tok[s] = bank_tok[3]
            c.dve.drain()
            a1_tok = bank_tok[3]
            wv_tok[0] = wv_tok[1] = a1_tok
            for ti in range(4):
                i = 4 * ci + ti
                bi = i % NB
                c.dve.wait(otoks[bi])
                tl = c.qb.one(xb[bi][:, :], x_in[128 * i:128 * (i + 1), :])
                mean, rstd, mrs = ln_stats(c, a1b[ti][:, :], lns2[ti % NB])
                c.dve.wait(tl)
                c.dve.raw("tensor_scalar", x1b[bi][:, :], xb[bi][:, :], mrs,
                          None, ALU.subtract)
                c.dve.drain()
                c.dve.raw("scalar_tensor_tensor", x1b[bi][:, :], a1b[ti][:, :],
                          rstd, x1b[bi][:, :], ALU.mult, ALU.add)
                c.dve.drain()
                t_x1 = c.dve.inc()
                c.sync.wait(t_x1)
                otoks[bi] = c.qb.one(x_out[128 * i:128 * (i + 1), :],
                                     x1b[bi][:, :])
            prev_chunk_tok = a1_tok
    return [t for t in otoks if t is not None]




def stage_D(c, x_in, x_out, W, W16, ses):
    nc = c.nc
    CH = 512
    NCHUNK = T // CH
    tok_d = nc.dram_tensor("d_tok", [8, 128, T], BF16, kind="Internal").ap()
    t4_d = nc.dram_tensor("d_t4", [T, 1024], F32, kind="Internal").ap()
    nwbc = c.sb("d_nwbc", [128, 1024], F32, es=ses)
    t_nw = c.qa.one(nwbc[:, :],
                    W["d_nw"].rearrange("(o f) -> o f", o=1).partition_broadcast(128))
    gate_sb = [c.sb(f"d_gate{k}", [128, 8], BF16, es=ses) for k in range(8)]
    c.qw.begin()
    for k in range(8):
        c.qw.dma(gate_sb[k][:, :], W["d_gateT"][128 * k:128 * (k + 1), :])
    t_gw = c.qw.end()
    b2sb = c.sb("d_b2sb", [8, 1024], BF16, es=ses)
    t_b2 = c.qw.one(b2sb[:, :], W["d_b2"][:, :])
    b1c = c.sb("d_b1c", [128, 128], F32, es=ses)
    c.qa.begin()
    for e in range(8):
        for j in range(16):
            c.qa.dma(b1c[:, 16 * e + j:16 * e + j + 1],
                     W["d_b1"][e, 128 * j:128 * (j + 1)].rearrange("(p o) -> p o", o=1))
    t_b1 = c.qa.end()
    zeros8 = c.sb("d_z8", [128, 8], F32, es=ses)
    c.dve.raw("memset", zeros8[:, :], 0.0)
    c.dve.drain()
    t_z8 = c.dve.inc()
    c.barrier([t_nw, t_gw, t_b2, t_b1, t_z8])

    comb = [c.sb(f"d_cmb{i}", [128, 8], F32, es=ses) for i in range(NT)]
    cmbT = c.sb("d_cmbT", [8, T], BF16, es=ses)
    gsb = c.sb("d_gsb", [128, 8], F32, es=ses)
    r_sc = {nm: c.sb(f"d_r{nm}", [128, lw], F32, es=ses) for nm, lw in
            (("m1n", 1), ("m2", 1), ("eq", 8), ("cum", 8), ("c1e", 8),
             ("fst2", 8), ("a", 1), ("den", 1), ("w1r", 1), ("w2r", 1),
             ("msk", 8))}

    with ExitStack() as pes:
        xb = [c.sb(f"d_xb{i}", [128, 1024], F32, es=pes) for i in range(2)]
        tb = [c.sb(f"d_tb{i}", [128, 1024], BF16, es=pes) for i in range(2)]
        t4b = [c.sb(f"d_t4b{i}", [128, 1024], F32, es=pes) for i in range(2)]
        lns = [mk_ln_scratch(c, f"D{i}", es=pes) for i in range(2)]
        tfm = [c.sb(f"d_tfm{i}", [128, 8 * 128], BF16, es=pes) for i in range(2)]
        ps_tr = pes.enter_context(nc.psum_tensor("d_pstr", [128, 2048], BF16))
        ps_g = pes.enter_context(nc.psum_tensor("d_psg", [128, 8], F32))
        ps_ct = pes.enter_context(nc.psum_tensor("d_psct", [8, 128], F32))
        ent_free = [None, None]
        half_tok = [None, None]
        tfm_free = [None, None]
        t4_tok = [None, None]
        g_tok = None
        ct_tok = None
        prev_a = None
        prev_cmb = None
        for i in range(NT):
            bi = i % 2
            c.sync.wait(ent_free[bi])
            c.sync.wait(t4_tok[bi])
            tl = c.qb.one(xb[bi][:, :], x_in[128 * i:128 * (i + 1), :])
            d = c.dve
            d.wait(tl)
            mean, rstd, mrs = ln_stats(c, xb[bi][:, :], lns[bi])
            d.raw("tensor_scalar", t4b[bi][:, :], xb[bi][:, :],
                  mean, rstd, ALU.subtract, ALU.mult)
            d.drain()
            t_t4 = d.inc()
            c.sync.wait(t_t4)
            t4_tok[bi] = c.qb.one(t4_d[128 * i:128 * (i + 1), :], t4b[bi][:, :])
            _, rstd2, _ = ln_stats2(c, t4b[bi][:, :], lns[bi], eps=1e-6)
            d.raw("scalar_tensor_tensor", tb[bi][:, :], t4b[bi][:, :],
                  rstd2, nwbc[:, :], ALU.mult, ALU.mult)
            d.drain()
            t_tb = d.inc()
            ent_free[bi] = t_tb
            hf = i % 2
            c.te.wait(t_tb)
            c.te.wait(half_tok[hf])
            for k in range(8):
                c.te.raw("transpose",
                         ps_tr[:, hf * 1024 + k * 128:hf * 1024 + (k + 1) * 128],
                         tb[bi][:, k * 128:(k + 1) * 128], c.ident16[:, :])
            t_tr = c.te.inc()
            d.wait(t_tr)
            d.wait(tfm_free[bi])
            for k in range(8):
                d.raw("tensor_copy", tfm[bi][:, 128 * k:128 * (k + 1)],
                      ps_tr[:, hf * 1024 + k * 128:hf * 1024 + (k + 1) * 128])
            d.drain()
            t_ev = d.inc()
            half_tok[hf] = t_ev
            c.sync.wait(t_ev)
            tfm_free[bi] = c.qa.one(
                tok_d[:, :, 128 * i:128 * (i + 1)].rearrange("a p t -> p a t"),
                tfm[bi][:, :].rearrange("p (a t) -> p a t", t=128))
            c.te.wait(t_ev)
            c.te.wait(g_tok)
            for k in range(8):
                c.te.raw("matmul", ps_g[:, :], tfm[bi][:, 128 * k:128 * (k + 1)],
                         gate_sb[k][:, :], start=(k == 0), stop=(k == 7))
            t_g = c.te.inc()
            d.wait(t_g)
            d.wait(prev_a)
            d.raw("tensor_copy", gsb[:, :], ps_g[:, :])
            d.raw("reduce_max", r_sc["m1n"][:, :], ps_g[:, :], axis=AX.X,
                  negate=True)
            g_tok = d.inc()
            d.drain()
            d.raw("tensor_scalar", r_sc["eq"][:, :], gsb[:, :],
                  r_sc["m1n"][:, 0:1], -1.0, ALU.add, ALU.mult)
            d.drain()
            d.raw("tensor_scalar", r_sc["eq"][:, :], r_sc["eq"][:, :], 0.0,
                  None, ALU.is_equal)
            d.drain()
            d.raw("tensor_tensor_scan", r_sc["cum"][:, :], r_sc["eq"][:, :],
                  zeros8[:, :], 0.0, ALU.add, ALU.add)
            d.drain()
            d.raw("tensor_scalar", r_sc["c1e"][:, :], r_sc["cum"][:, :],
                  1.0, None, ALU.is_equal)
            d.drain()
            d.raw("tensor_tensor", comb[i][:, :], r_sc["eq"][:, :],
                  r_sc["c1e"][:, :], ALU.mult)
            d.drain()
            d.raw("scalar_tensor_tensor", r_sc["msk"][:, :], comb[i][:, :],
                  -1e30, gsb[:, :], ALU.mult, ALU.add)
            d.drain()
            d.raw("reduce_max", r_sc["m2"][:, :], r_sc["msk"][:, :], axis=AX.X)
            d.drain()
            d.raw("tensor_scalar", r_sc["eq"][:, :], r_sc["msk"][:, :],
                  r_sc["m2"][:, 0:1], None, ALU.is_equal)
            d.drain()
            d.raw("tensor_tensor_scan", r_sc["cum"][:, :], r_sc["eq"][:, :],
                  zeros8[:, :], 0.0, ALU.add, ALU.add)
            d.drain()
            d.raw("tensor_scalar", r_sc["c1e"][:, :], r_sc["cum"][:, :],
                  1.0, None, ALU.is_equal)
            d.drain()
            d.raw("tensor_tensor", r_sc["fst2"][:, :], r_sc["eq"][:, :],
                  r_sc["c1e"][:, :], ALU.mult)
            t_route = d.inc()
            c.act.wait(t_route)
            c.act.wait(prev_cmb)
            c.act.raw("activation", r_sc["a"][:, 0:1], r_sc["m2"][:, 0:1],
                      AF.Exp, bias=r_sc["m1n"][:, 0:1], scale=1.0)
            prev_a = c.act.inc()
            d.wait(prev_a)
            d.raw("tensor_scalar", r_sc["den"][:, :], r_sc["a"][:, :], 1.0,
                  None, ALU.add)
            d.drain()
            d.raw("reciprocal", r_sc["w1r"][:, :], r_sc["den"][:, :])
            d.drain()
            d.raw("tensor_tensor", r_sc["w2r"][:, :], r_sc["a"][:, :],
                  r_sc["w1r"][:, :], ALU.mult)
            d.drain()
            d.raw("tensor_scalar", comb[i][:, :], comb[i][:, :],
                  r_sc["w1r"][:, 0:1], None, ALU.mult)
            d.drain()
            d.raw("scalar_tensor_tensor", comb[i][:, :], r_sc["fst2"][:, :],
                  r_sc["w2r"][:, 0:1], comb[i][:, :], ALU.mult, ALU.add)
            d.drain()
            prev_cmb = d.inc()
            c.te.wait(prev_cmb)
            c.te.wait(ct_tok)
            c.te.raw("transpose", ps_ct[:, :], comb[i][:, :], c.identf[:, :])
            t_ctm = c.te.inc()
            d.wait(t_ctm)
            d.raw("tensor_copy", cmbT[:, 128 * i:128 * (i + 1)], ps_ct[0:8, :])
            d.drain()
            ct_tok = d.inc()
        c.barrier([ct_tok] + [t for t in tfm_free if t] +
                  [t for t in t4_tok if t])

    with ExitStack() as pes:
        tok_fm = [[c.sb(f"d_tok{p}_{k}", [128, CH], BF16, es=pes)
                   for k in range(8)] for p in range(2)]
        acc = [[c.sb(f"d_acc{p}_{i}", [128, 1024], F32, es=pes)
                for i in range(4)] for p in range(2)]
        h_fm = [c.sb(f"d_h{k}", [128, CH], BF16, es=pes) for k in range(16)]
        w1buf = [c.sb(f"d_w1{k}", [128, 2048], BF16, es=pes) for k in range(8)]
        w2buf = [c.sb(f"d_w2{k}", [128, 1024], BF16, es=pes) for k in range(16)]
        xbE = [c.sb(f"d_xbE{i}", [128, 1024], F32, es=pes) for i in range(2)]
        x1b = [c.sb(f"d_x1b{i}", [128, 1024], F32, es=pes) for i in range(2)]
        lns2 = [mk_ln_scratch(c, f"D2{i}", es=pes) for i in range(2)]
        ps = [pes.enter_context(nc.psum_tensor(f"d_ps{i}", [128, 512], F32))
              for i in range(4)]
        bank_tok = [None] * 4
        st = {"w1_tok": None, "w2_tok": None, "otok": [None, None],
              "ld": [None, None], "acc_free": [None, None]}

        def load_chunk(ci):
            p = ci % 2
            c.sync.wait(st["ld"][p])
            c.qa.begin()
            for k in range(8):
                c.qa.dma(tok_fm[p][k][:, :], tok_d[k, :, CH * ci:CH * (ci + 1)])
            t_tok = c.qa.end()
            c.sync.wait(st["acc_free"][p])
            c.qb.begin()
            for ti in range(4):
                i = 4 * ci + ti
                c.qb.dma(acc[p][ti][:, :], t4_d[128 * i:128 * (i + 1), :])
            t_acc = c.qb.end()
            return t_tok, t_acc

        def exit_tile(ci, ti):
            p = ci % 2
            i = 4 * ci + ti
            bi = i % 2
            d = c.dve
            d.wait(st["otok"][bi])
            tl = c.qb.one(xbE[bi][:, :], x_in[128 * i:128 * (i + 1), :])
            mean, rstd, mrs = ln_stats(c, acc[p][ti][:, :], lns2[bi])
            d.wait(tl)
            d.raw("tensor_scalar", x1b[bi][:, :], xbE[bi][:, :], mrs,
                  None, ALU.subtract)
            d.drain()
            d.raw("scalar_tensor_tensor", x1b[bi][:, :], acc[p][ti][:, :],
                  rstd, x1b[bi][:, :], ALU.mult, ALU.add)
            d.drain()
            t_x1 = d.inc()
            c.sync.wait(t_x1)
            st["otok"][bi] = c.qb.one(x_out[128 * i:128 * (i + 1), :],
                                      x1b[bi][:, :])
            if ti == 3:
                st["acc_free"][p] = t_x1
            return st["otok"][bi]

        ld_tok = {0: load_chunk(0)}
        final = []
        for ci in range(NCHUNK):
            p = ci % 2
            t_tok, t_acc = ld_tok.pop(ci)
            c.te.wait(t_tok)
            for e in range(8):
                c.sync.wait(st["w1_tok"])
                c.qa.begin()
                for k in range(8):
                    c.qa.dma(w1buf[k][:, :],
                             W16["w1T"][1024 * e + 128 * k:1024 * e + 128 * (k + 1), :])
                t_w1 = c.qa.end()
                c.sync.wait(st["w2_tok"])
                c.qa.begin()
                for k in range(16):
                    c.qa.dma(w2buf[k][:, :],
                             W16["w2T"][2048 * e + 128 * k:2048 * e + 128 * (k + 1), :])
                t_w2 = c.qa.end()
                c.te.wait(t_w1)
                for j in range(16):
                    pb_ = j % 4
                    c.te.wait(bank_tok[pb_])
                    for k in range(8):
                        c.te.raw("matmul", ps[pb_][:, :],
                                 w1buf[k][:, 128 * j:128 * (j + 1)],
                                 tok_fm[p][k][:, :], start=(k == 0), stop=(k == 7))
                    t_mm = c.te.inc()
                    c.act.wait(t_mm)
                    c.act.raw("activation", h_fm[j][:, :], ps[pb_][:, :],
                              AF.Gelu, bias=b1c[:, 16 * e + j:16 * e + j + 1],
                              scale=1.0)
                    bank_tok[pb_] = c.act.inc()
                st["w1_tok"] = bank_tok[15 % 4]
                c.te.wait(st["w1_tok"])
                c.te.wait(t_w2)
                for ti in range(4):
                    for jc in range(2):
                        pb_ = (2 * ti + jc) % 4
                        c.te.wait(bank_tok[pb_])
                        if e == 0 and ti == 0 and jc == 0:
                            c.dve.wait(t_acc)
                        for k in range(16):
                            c.te.raw("matmul", ps[pb_][:, :],
                                     h_fm[k][:, 128 * ti:128 * (ti + 1)],
                                     w2buf[k][:, 512 * jc:512 * (jc + 1)],
                                     start=(k == 0), stop=(k == 15))
                        t_mm = c.te.inc()
                        c.dve.wait(t_mm)
                        c.dve.raw("scalar_tensor_tensor",
                                  acc[p][ti][:, 512 * jc:512 * (jc + 1)],
                                  ps[pb_][:, :], comb[4 * ci + ti][:, e:e + 1],
                                  acc[p][ti][:, 512 * jc:512 * (jc + 1)],
                                  ALU.mult, ALU.add)
                        c.dve.drain()
                        bank_tok[pb_] = c.dve.inc()
                st["w2_tok"] = bank_tok[(2 * 3 + 1) % 4]
            for ti in range(4):
                for jc in range(2):
                    pb_ = (2 * ti + jc) % 4
                    c.te.wait(bank_tok[pb_])
                    c.te.raw("matmul", ps[pb_][:, :],
                             cmbT[:, 128 * (4 * ci + ti):128 * (4 * ci + ti + 1)],
                             b2sb[:, 512 * jc:512 * (jc + 1)],
                             start=True, stop=True)
                    t_mm = c.te.inc()
                    c.dve.wait(t_mm)
                    c.dve.raw("tensor_tensor",
                              acc[p][ti][:, 512 * jc:512 * (jc + 1)],
                              ps[pb_][:, :],
                              acc[p][ti][:, 512 * jc:512 * (jc + 1)], ALU.add)
                    c.dve.drain()
                    bank_tok[pb_] = c.dve.inc()
            st["ld"][p] = bank_tok[(2 * 3 + 1) % 4]
            for ti in range(4):
                final.append(exit_tile(ci, ti))
            if ci + 1 < NCHUNK:
                ld_tok[ci + 1] = load_chunk(ci + 1)
    return final


def stage_C(c, x_in, x_out, W, ses):
    nc = c.nc
    CH = 512
    NCHUNK = T // CH
    zxb_d = nc.dram_tensor("c_zxb", [34, 128, T], BF16, kind="Internal").ap()
    t3_d = nc.dram_tensor("c_t3", [T, 1024], F32, kind="Internal").ap()
    u_d = nc.dram_tensor("c_u", [T, 1024], F32, kind="Internal").ap()
    y_d = nc.dram_tensor("c_y", [16, 128, T], BF16, kind="Internal").ap()
    s_d = nc.dram_tensor("c_s", [T // 128, 32, 128], F32, kind="Internal").ap()
    exps_d = nc.dram_tensor("c_exps", [T // 128, 32, 128], BF16,
                            kind="Internal").ap()

    dtc = c.sb("c_dtbc", [32, 1], F32, es=ses)
    alc = c.sb("c_alc", [32, 1], F32, es=ses)
    nAc = c.sb("c_nA", [32, 1], F32, es=ses)
    Drow = c.sb("c_Drow", [1, 32], F32, es=ses)
    Dbc = c.sb("c_Dbc", [128, 32], F32, es=ses)
    onesf = c.sb("c_onesf", [1, 128], F32, es=ses)
    ones16 = c.sb("c_ones16", [1, 128], BF16, es=ses)
    tri16 = c.sb("c_tri16", [128, 128], BF16, es=ses)
    n1bc = c.sb("c_n1bc", [128, 1024], F32, es=ses)
    n2bc = c.sb("c_n2bc", [128, 1024], F32, es=ses)
    mwc = c.sb("c_mwc", [128, 16], F32, es=ses)
    mb2bc = c.sb("c_mb2bc", [128, 1024], F32, es=ses)
    mb1c = c.sb("c_mb1c", [128, 32], F32, es=ses)
    wcc = c.sb("c_wcc", [128, 72], F32, es=ses)
    cbc = c.sb("c_cbc", [128, 18], F32, es=ses)


    c.qa.begin()
    c.qa.dma(dtc[:, :], W["c_dtb"].rearrange("(p o) -> p o", o=1))
    c.qa.dma(alc[:, :], W["c_Alog"].rearrange("(p o) -> p o", o=1))
    c.qa.dma(Drow[:, :], W["c_D"].rearrange("(o p) -> o p", o=1))
    c.qa.dma(n1bc[:, :], W["c_n1w"].rearrange("(o f) -> o f", o=1).partition_broadcast(128))
    c.qa.dma(n2bc[:, :], W["c_n2w"].rearrange("(o f) -> o f", o=1).partition_broadcast(128))
    c.qa.dma(mb2bc[:, :], W["c_mb2"].rearrange("(o f) -> o f", o=1).partition_broadcast(128))
    for j in range(16):
        c.qa.dma(mwc[:, j:j + 1], W["c_mnw"][128 * j:128 * (j + 1)].rearrange("(p o) -> p o", o=1))
    for j in range(32):
        c.qa.dma(mb1c[:, j:j + 1], W["c_mb1"][128 * j:128 * (j + 1)].rearrange("(p o) -> p o", o=1))
    for tl in range(18):
        c.qa.dma(wcc[:, 4 * tl:4 * (tl + 1)], W["c_convw"][128 * tl:128 * (tl + 1), :])
        c.qa.dma(cbc[:, tl:tl + 1], W["c_convb"][128 * tl:128 * (tl + 1)].rearrange("(p o) -> p o", o=1))
    t_cst = c.qa.end()
    d = c.dve
    d.raw("memset", onesf[:, :], 1.0)
    d.drain()
    d.raw("tensor_copy", ones16[:, :], onesf[:, :])
    d.raw("tensor_scalar", tri16[:, :], c.iotaf[:, :], c.iotap[:, 0:1],
          None, ALU.is_ge)
    d.drain()
    t_ones = d.inc()
    c.act.wait(t_cst)
    c.act.raw("activation", nAc[:, 0:1], alc[:, 0:1], AF.Exp, bias=0.0, scale=1.0)
    t_na = c.act.inc()
    d.wait(t_na)
    d.raw("tensor_scalar", nAc[:, 0:1], nAc[:, 0:1], -1.0, None, ALU.mult)
    d.drain()
    t_na = d.inc()
    c.barrier([t_cst, t_ones, t_na])

    NB = 2
    s_es = ExitStack()
    s_sb = c.sb("c_ssb", [32, T], F32, es=s_es)
    nst_sb = c.sb("c_nst", [32, T], F32, es=s_es)
    dt_es = ExitStack()
    dt_sb = c.sb("c_dt", [32, T], F32, es=dt_es)
    with ExitStack() as pes:
        inpT = [c.sb(f"c_inpT{k}", [128, 4384], BF16, es=pes) for k in range(8)]
        c.qw.begin()
        for k in range(8):
            c.qw.dma(inpT[k][:, :], W["c_inpT"][128 * k:128 * (k + 1), :])
        t_w = c.qw.end()
        xb = [c.sb(f"c_xb{i}", [128, 1024], F32, es=pes) for i in range(NB)]
        t3b = [c.sb(f"c_t3b{i}", [128, 1024], F32, es=pes) for i in range(NB)]
        nb = [c.sb(f"c_nb{i}", [128, 1024], BF16, es=pes) for i in range(NB)]
        lns = [mk_ln_scratch(c, f"C{i}", es=pes) for i in range(NB)]
        n1_fm = [c.sb(f"c_n1fm{k}", [128, CH], BF16, es=pes) for k in range(8)]
        zsb = [c.sb(f"c_zsb{i}", [128, CH], BF16, es=pes) for i in range(4)]
        ps_tr = pes.enter_context(nc.psum_tensor("c_pstr", [128, 2048], BF16))
        ps = [pes.enter_context(nc.psum_tensor(f"c_ps{i}", [128, 512], F32))
              for i in range(4)]
        entry_tok = [None] * NB
        t3_tok = [None] * NB
        zdma = [None] * 4
        bank_tok = [None] * 4
        prev_chunk = None
        c.te.wait(t_w)
        for ci in range(NCHUNK):
            for ti in range(4):
                i = 4 * ci + ti
                bi = i % NB
                c.sync.wait(entry_tok[bi])
                c.sync.wait(t3_tok[bi])
                tl = c.qb.one(xb[bi][:, :], x_in[128 * i:128 * (i + 1), :])
                c.dve.wait(tl)
                mean, rstd, mrs = ln_stats(c, xb[bi][:, :], lns[bi])
                c.dve.raw("tensor_scalar", t3b[bi][:, :], xb[bi][:, :],
                          mean, rstd, ALU.subtract, ALU.mult)
                c.dve.drain()
                t_t3 = c.dve.inc()
                c.sync.wait(t_t3)
                t3_tok[bi] = c.qb.one(t3_d[128 * i:128 * (i + 1), :],
                                      t3b[bi][:, :])
                _, rstd1, _ = ln_stats2(c, t3b[bi][:, :], lns[bi], eps=1e-6,
                                        scale=1024.0)
                c.dve.raw("scalar_tensor_tensor", nb[bi][:, :], t3b[bi][:, :],
                          rstd1, n1bc[:, :], ALU.mult, ALU.mult)
                c.dve.drain()
                t_nb = c.dve.inc()
                hf = i % 2
                c.te.wait(t_nb)
                c.te.wait(prev_chunk)
                for k in range(8):
                    c.te.raw("transpose",
                             ps_tr[:, hf * 1024 + k * 128:hf * 1024 + (k + 1) * 128],
                             nb[bi][:, k * 128:(k + 1) * 128], c.ident16[:, :])
                t_tr = c.te.inc()
                c.dve.wait(t_tr)
                for k in range(8):
                    c.dve.raw("tensor_copy", n1_fm[k][:, 128 * ti:128 * (ti + 1)],
                              ps_tr[:, hf * 1024 + k * 128:hf * 1024 + (k + 1) * 128])
                entry_tok[bi] = c.dve.inc()
            for t in entry_tok:
                c.te.wait(t)
            for j in range(35):
                jw = 128 if j < 34 else 32
                pb_ = j % 4
                c.te.wait(bank_tok[pb_])
                for k in range(8):
                    c.te.raw("matmul", ps[pb_][0:jw, :],
                             inpT[k][:, 128 * j:128 * j + jw],
                             n1_fm[k][:, :], start=(k == 0), stop=(k == 7))
                t_mm = c.te.inc()
                c.dve.wait(t_mm)
                if j < 34:
                    sj = j % 4
                    c.dve.wait(zdma[sj])
                    c.dve.raw("tensor_copy", zsb[sj][:, :], ps[pb_][:, :])
                    c.dve.drain()
                    bank_tok[pb_] = c.dve.inc()
                    c.sync.wait(bank_tok[pb_])
                    zdma[sj] = c.qa.one(zxb_d[j, :, CH * ci:CH * (ci + 1)],
                                        zsb[sj][:, :])
                else:
                    c.dve.raw("tensor_copy", dt_sb[:, CH * ci:CH * (ci + 1)],
                              ps[pb_][0:32, :])
                    c.dve.drain()
                    bank_tok[pb_] = c.dve.inc()
            prev_chunk = bank_tok[34 % 4]
        c.barrier([t for t in bank_tok if t is not None] +
                  [t for t in zdma if t is not None] +
                  [t for t in t3_tok if t is not None] +
                  [t for t in entry_tok if t is not None])

    one_col, one_tok = c.const_col(1.0)
    c.act.wait(one_tok)
    with ExitStack() as pes:
        ab = c.sb("c_ab", [32, T], F32, es=pes)
        exps16 = c.sb("c_e16", [32, T], BF16, es=pes)
        zeros32 = c.sb("c_z32", [32, 128], F32, es=pes)
        d.raw("tensor_scalar", dt_sb[:, :], dt_sb[:, :], dtc[:, 0:1], None,
              ALU.add)
        d.drain()
        t_ = d.inc()
        c.act.wait(t_)
        c.act.raw("activation", ab[:, :], dt_sb[:, :], AF.Abs, bias=0.0, scale=1.0)
        c.act.drain()
        c.act.raw("activation", ab[:, :], ab[:, :], AF.Exp, bias=0.0, scale=-1.0)
        c.act.drain()
        c.act.raw("activation", ab[:, :], ab[:, :], AF.Ln,
                  bias=one_col[0:32, 0:1], scale=1.0)
        t_ln = c.act.inc()
        d.wait(t_ln)
        d.raw("tensor_scalar", dt_sb[:, :], dt_sb[:, :], 0.0, None, ALU.max)
        d.drain()
        d.raw("tensor_tensor", dt_sb[:, :], dt_sb[:, :], ab[:, :], ALU.add)
        d.drain()
        t_dt = d.inc()
        d.raw("tensor_scalar", ab[:, :], dt_sb[:, :], nAc[:, 0:1], None, ALU.mult)
        d.drain()
        d.raw("memset", zeros32[:, :], 0.0)
        d.drain()
        for cc in range(T // 128):
            d.raw("tensor_tensor_scan", s_sb[:, 128 * cc:128 * (cc + 1)],
                  ab[:, 128 * cc:128 * (cc + 1)], zeros32[:, :], 0.0,
                  ALU.add, ALU.add)
        d.drain()
        t_s = d.inc()
        c.act.wait(t_dt)
        c.act.wait(t_s)
        c.act.raw("activation", nst_sb[:, :], dt_sb[:, :], AF.Ln, bias=0.0,
                  scale=1.0)
        c.act.drain()
        c.act.raw("activation", exps16[:, :], s_sb[:, :], AF.Exp, bias=0.0,
                  scale=1.0)
        t_e = c.act.inc()
        d.wait(t_e)
        d.raw("tensor_tensor", nst_sb[:, :], nst_sb[:, :], s_sb[:, :],
              ALU.subtract)
        d.drain()
        t_nst = d.inc()
        c.sync.wait(t_s)
        c.sync.wait(t_e)
        c.sync.wait(t_nst)
        c.qa.begin()
        c.qa.dma(s_d.rearrange("c h i -> h c i"),
                 s_sb.rearrange("h (c i) -> h c i", i=128))
        c.qa.dma(exps_d.rearrange("c h i -> h c i"),
                 exps16.rearrange("h (c i) -> h c i", i=128))
        t_sd = c.qa.end()
        with nc.psum_tensor("c_psg", [128, 128], F32) as ps_g:
            c.te.wait(t_cst)
            c.te.wait(t_ones)
            c.te.raw("matmul", ps_g[:, 0:32], onesf[:, :], Drow[:, :],
                     start=True, stop=True)
            t_g = c.te.inc()
            d.wait(t_g)
            d.raw("tensor_copy", Dbc[:, :], ps_g[:, 0:32])
            d.drain()
            t_dbc = d.inc()
        c.barrier([t_sd, t_nst, t_dbc])
    dt_es.close()

    with ExitStack() as pes:
        NP = 4
        xcb = [c.sb(f"c_xcb{t}", [128, CH + 3], BF16, es=pes) for t in range(18)]
        cvb = [c.sb(f"c_cvb{i}", [128, CH], F32, es=pes) for i in range(2)]
        xbc = [c.sb(f"c_xbc{t}", [128, CH], BF16, es=pes) for t in range(18)]
        xt_sb = c.sb("c_xt", [128, 2048], BF16, es=pes)
        btm = c.sb("c_btm", [128, 128], BF16, es=pes)
        nstT = c.sb("c_nstT", [128, 32], F32, es=pes)
        wT = c.sb("c_wT", [128, 32], F32, es=pes)
        gam = c.sb("c_gam", [128, 32], F32, es=pes)
        grow = c.sb("c_grow", [1, 32], BF16, es=pes)
        wbuf = c.sb("c_wbuf", [32, 128], F32, es=pes)
        sbc = [c.sb(f"c_sbc{i}", [128, 4096], F32, es=pes) for i in range(2)]
        ebc = [c.sb(f"c_ebc{i}", [128, 4096], BF16, es=pes) for i in range(2)]
        S_all = c.sb("c_S", [128, 2048], F32, es=pes)
        S16 = c.sb("c_S16", [128, 2048], BF16, es=pes)
        CBt = c.sb("c_CBt", [128, 128], BF16, es=pes)
        d2sb = [c.sb(f"c_d2{i}", [128, 128], F32, es=pes) for i in range(NP)]
        L16 = [c.sb(f"c_L16{i}", [128, 128], BF16, es=pes) for i in range(NP)]
        M16 = [c.sb(f"c_M16{i}", [128, 128], BF16, es=pes) for i in range(NP)]
        xw16 = [c.sb(f"c_xw{i}", [128, 64], BF16, es=pes) for i in range(NP)]
        cpp = [c.sb(f"c_cpp{i}", [128, 128], BF16, es=pes) for i in range(NP)]
        ystg = c.sb("c_ystg", [128, 2048], BF16, es=pes)
        ps_t16 = pes.enter_context(nc.psum_tensor("c_pst16", [128, 1024], BF16))
        ps_misc = pes.enter_context(nc.psum_tensor("c_psmisc", [128, 512], F32))
        ps_cb = ps_misc[:, 96:224]
        ps_y = [pes.enter_context(nc.psum_tensor(f"c_psy{i}", [64, 128], F32))
                for i in range(NP)]
        ps_ds = [pes.enter_context(nc.psum_tensor(f"c_psds{i}", [128, 64], F32))
                 for i in range(2)]

        conv_tok = [None] * 18
        t_d2f = [None] * NP
        t_Lf = [None] * NP
        t_Mf = [None] * NP
        t_yf = [None] * NP
        t_dsf = [None, None]
        t_s16 = None
        t_row = [None, None]
        t_pre = None
        t_ystg = None
        ydma = None
        last_y = None
        for ci in range(NCHUNK):
            first = (ci % 4 == 0)
            for tl in range(18):
                c.sync.wait(conv_tok[tl])
            c.sync.wait(last_y)
            c.qa.begin()
            for tl in range(18):
                if first:
                    c.qa.dma(xcb[tl][:, 3:CH + 3],
                             zxb_d[16 + tl, :, CH * ci:CH * (ci + 1)])
                else:
                    c.qa.dma(xcb[tl][:, 0:CH + 3],
                             zxb_d[16 + tl, :, CH * ci - 3:CH * (ci + 1)])
            t_x = c.qa.end()
            for tl in range(18):
                cb_ = cvb[tl % 2]
                c.dve.wait(t_x)
                c.dve.wait(last_y)
                if tl >= 2:
                    c.dve.wait(conv_tok[tl - 2])
                elif ci > 0:
                    c.dve.wait(conv_tok[16 + tl])
                if first:
                    c.dve.raw("memset", xcb[tl][:, 0:3], 0.0)
                    c.dve.drain()
                c.dve.raw("tensor_scalar", cb_[:, :], xcb[tl][:, 0:CH],
                          wcc[:, 4 * tl:4 * tl + 1], None, ALU.mult)
                c.dve.drain()
                for kk in range(1, 4):
                    c.dve.raw("scalar_tensor_tensor", cb_[:, :],
                              xcb[tl][:, kk:CH + kk],
                              wcc[:, 4 * tl + kk:4 * tl + kk + 1],
                              cb_[:, :], ALU.mult, ALU.add)
                    c.dve.drain()
                t_cv = c.dve.inc()
                c.act.wait(t_cv)
                c.act.wait(conv_tok[tl])
                c.act.wait(last_y)
                c.act.raw("activation", xbc[tl][:, :], cb_[:, :], AF.Silu,
                          bias=cbc[:, tl:tl + 1], scale=1.0)
                conv_tok[tl] = c.act.inc()
            t_conv = conv_tok[17]
            c.te.wait(t_conv)
            for sub in range(4):
                cc = 4 * ci + sub
                lcol = 128 * sub
                cend = 128 * cc + 127
                pi = cc % 2
                c.sync.wait(t_row[pi])
                c.qa.begin()
                c.qa.dma(sbc[pi].rearrange("p (a b) -> p a b", b=128),
                         s_d[cc].partition_broadcast(128))
                c.qa.dma(ebc[pi].rearrange("p (a b) -> p a b", b=128),
                         exps_d[cc].partition_broadcast(128))
                t_rows = c.qa.end()
                if cc % 16 == 0:
                    c.dve.wait(t_s16)
                    c.dve.raw("memset", S_all[:, :], 0.0)
                    c.dve.drain()
                    c.dve.raw("memset", S16[:, :], 0.0)
                    c.dve.drain()
                    t_srst = c.dve.inc()
                    c.te.wait(t_srst)
                    t_s16 = t_srst
                c.te.wait(t_pre)
                c.te.wait(t_s16)
                c.te.raw("transpose", ps_t16[:, 0:128],
                         xbc[16][:, lcol:lcol + 128], c.ident16[:, :])
                t_btr = c.te.inc()
                c.dve.wait(t_btr)
                c.dve.raw("tensor_copy", btm[:, :], ps_t16[:, 0:128])
                c.dve.drain()
                t_btm = c.dve.inc()
                c.te.wait(t_btm)
                for g2 in range(2):
                    for kk in range(8):
                        k = 8 * g2 + kk
                        c.te.raw("transpose", ps_t16[:, kk * 128:(kk + 1) * 128],
                                 xbc[k][:, lcol:lcol + 128], c.ident16[:, :])
                    t_xtr = c.te.inc()
                    c.dve.wait(t_xtr)
                    for kk in range(8):
                        k = 8 * g2 + kk
                        c.dve.raw("tensor_copy", xt_sb[:, 128 * k:128 * (k + 1)],
                                  ps_t16[:, kk * 128:(kk + 1) * 128])
                    c.dve.drain()
                    t_xt = c.dve.inc()
                    if g2 == 0:
                        c.te.wait(t_xt)
                c.te.raw("transpose", ps_misc[:, 0:32],
                         nst_sb[:, 128 * cc:128 * (cc + 1)], c.identf[0:32, 0:32])
                t_ntr = c.te.inc()
                c.dve.wait(t_ntr)
                c.dve.raw("tensor_copy", nstT[:, :], ps_misc[:, 0:32])
                c.dve.drain()
                t_nstT = c.dve.inc()
                c.dve.raw("tensor_scalar", wbuf[:, :],
                          nst_sb[:, 128 * cc:128 * (cc + 1)],
                          s_sb[:, cend:cend + 1], None, ALU.add)
                c.dve.drain()
                t_wb = c.dve.inc()
                c.act.wait(t_wb)
                c.act.raw("activation", wbuf[:, :], wbuf[:, :], AF.Exp,
                          bias=0.0, scale=1.0)
                t_wexp = c.act.inc()
                c.te.wait(t_wexp)
                c.te.wait(t_nstT)
                c.te.raw("transpose", ps_misc[:, 32:64], wbuf[:, :],
                         c.identf[0:32, 0:32])
                t_wtr = c.te.inc()
                c.dve.wait(t_wtr)
                c.dve.raw("tensor_copy", wT[:, :], ps_misc[:, 32:64])
                c.dve.drain()
                t_wT = c.dve.inc()
                c.dve.wait(t_rows)
                c.dve.raw("tensor_copy", grow[0:1, :],
                          ebc[pi].rearrange("p (a b) -> p a b", b=128)[0:1, :, 127])
                c.dve.drain()
                t_grow = c.dve.inc()
                c.te.wait(t_grow)
                c.te.wait(t_wT)
                c.te.raw("matmul", ps_misc[:, 224:256], ones16[:, :],
                         grow[0:1, :], start=True, stop=True)
                t_gb = c.te.inc()
                c.dve.wait(t_gb)
                c.dve.raw("tensor_copy", gam[:, :], ps_misc[:, 224:256])
                c.dve.drain()
                t_gam = c.dve.inc()
                c.te.wait(t_gam)
                c.te.raw("matmul", ps_cb[:, :], xbc[16][:, lcol:lcol + 128],
                         xbc[17][:, lcol:lcol + 128], start=True, stop=True)
                t_cbm = c.te.inc()
                c.dve.wait(t_cbm)
                for t_ in t_Mf:
                    c.dve.wait(t_)
                c.dve.raw("tensor_tensor", CBt[:, :], ps_cb[:, :], tri16[:, :],
                          ALU.mult)
                c.dve.drain()
                t_cbt = c.dve.inc()
                c.te.wait(t_rows)
                c.pool.wait(t_rows)
                c.pool.wait(t_wT)
                def front(h):
                    hp = h % NP
                    dd = c.dve
                    dd.wait(t_nstT)
                    dd.wait(t_Lf[hp])
                    dd.raw("tensor_scalar", d2sb[hp][:, :],
                           sbc[pi][:, 128 * h:128 * (h + 1)],
                           nstT[:, h:h + 1], 30.0, ALU.add, ALU.min)
                    t_d2 = dd.inc()
                    c.act.wait(t_d2)
                    c.act.wait(t_Mf[hp])
                    c.act.raw("activation", L16[hp][:, :], d2sb[hp][:, :],
                              AF.Exp, bias=0.0, scale=1.0)
                    t_L = c.act.inc()
                    t_d2f[hp] = t_L
                    t_Lf[hp] = t_L
                    c.pool.wait(t_dsf[h % 2])
                    c.pool.raw("tensor_scalar", xw16[hp][:, :],
                               xt_sb[:, 64 * h:64 * (h + 1)], wT[:, h:h + 1],
                               None, ALU.mult)
                    t_pl = c.pool.inc(rotate=True)
                    dd.wait(t_yf[hp])
                    dd.raw("tensor_tensor", cpp[hp][:, :],
                           xbc[17][:, lcol:lcol + 128],
                           ebc[pi][:, 128 * h:128 * (h + 1)], ALU.mult)
                    t_cp = dd.inc()
                    return t_L, t_pl, t_cp

                def mid(h, toks):
                    hp = h % NP
                    t_L, t_pl, t_cp = toks
                    dd = c.dve
                    dd.wait(t_L)
                    dd.wait(t_cbt)
                    dd.wait(t_yf[hp])
                    dd.raw("tensor_tensor", M16[hp][:, :], CBt[:, :],
                           L16[hp][:, :], ALU.mult)
                    tM = dd.inc()
                    t_Mf[hp] = tM
                    c.te.wait(tM)
                    c.te.wait(t_pl)
                    c.te.wait(t_cp)
                    c.te.wait(t_yf[hp])
                    c.te.raw("matmul", ps_y[hp][:, :],
                             xt_sb[:, 64 * h:64 * (h + 1)], M16[hp][:, :],
                             start=True, stop=False)
                    c.te.raw("matmul", ps_y[hp][:, :],
                             S16[:, 64 * h:64 * (h + 1)], cpp[hp][:, :],
                             start=False, stop=True)
                    t_ymm = c.te.inc()
                    c.te.wait(t_dsf[h % 2])
                    c.te.raw("matmul", ps_ds[h % 2][:, :], btm[:, :],
                             xw16[hp][:, :], start=True, stop=True)
                    t_dsm = c.te.inc()
                    return t_ymm, t_dsm

                def back(h, toks):
                    global_ = None
                    hp = h % NP
                    t_ymm, t_dsm = toks
                    dd = c.dve
                    dd.wait(t_ymm)
                    dd.wait(t_ystg)
                    dd.raw("scalar_tensor_tensor",
                           ystg[64 * (h % 2):64 * (h % 2) + 64,
                                128 * (h // 2):128 * (h // 2 + 1)],
                           xbc[h // 2][64 * (h % 2):64 * (h % 2) + 64,
                                       lcol:lcol + 128],
                           Dbc[64 * (h % 2):64 * (h % 2) + 64, h:h + 1],
                           ps_y[hp][:, :], ALU.mult, ALU.add)
                    t_y = dd.inc()
                    t_yf[hp] = t_y
                    dd.wait(t_dsm)
                    dd.wait(t_gam)
                    dd.raw("scalar_tensor_tensor", S_all[:, 64 * h:64 * (h + 1)],
                           S_all[:, 64 * h:64 * (h + 1)], gam[:, h:h + 1],
                           ps_ds[h % 2][:, :], ALU.mult, ALU.add)
                    t_su = dd.inc()
                    t_dsf[h % 2] = t_su
                    c.act.wait(t_su)
                    c.act.raw("copy", S16[:, 64 * h:64 * (h + 1)],
                              S_all[:, 64 * h:64 * (h + 1)])
                    return c.act.inc()

                ft = {}
                mt = {}
                for hh in range(34):
                    if hh < 32:
                        ft[hh] = front(hh)
                    if 1 <= hh <= 32:
                        mt[hh - 1] = mid(hh - 1, ft.pop(hh - 1))
                    if hh >= 2:
                        t_s16 = back(hh - 2, mt.pop(hh - 2))
                t_row[pi] = t_yf[(31) % NP]
                last_y = t_yf[(31) % NP]
                c.sync.wait(ydma)
                for t_ in t_yf:
                    c.sync.wait(t_)
                ydma = c.qa.one(
                    y_d[:, :, 128 * cc:128 * (cc + 1)].rearrange("a p t -> p a t"),
                    ystg[:, :].rearrange("p (a t) -> p a t", t=128))
                t_ystg = ydma
                t_pre = t_yf[(31) % NP]
        c.barrier([ydma, t_s16] + [t for t in t_yf if t is not None])
    s_es.close()

    with ExitStack() as pes:
        outT = [c.sb(f"c_outT{k}", [128, 1024], BF16, es=pes) for k in range(16)]
        c.qw.begin()
        for k in range(16):
            c.qw.dma(outT[k][:, :], W["c_outT"][128 * k:128 * (k + 1), :])
        t_w = c.qw.end()
        c.dve.wait(t_w)
        for k in range(16):
            c.dve.raw("tensor_scalar", outT[k][:, :], outT[k][:, :],
                      mwc[:, k:k + 1], None, ALU.mult)
        c.dve.drain()
        t_w = c.dve.inc()
        yb = [c.sb(f"c_yb{k}", [128, CH], BF16, es=pes) for k in range(16)]
        zb = [c.sb(f"c_zb{k}", [128, CH], BF16, es=pes) for k in range(16)]
        zs = [c.sb(f"c_zs{i}", [128, CH], BF16, es=pes) for i in range(2)]
        yg = [c.sb(f"c_yg{k}", [128, CH], BF16, es=pes) for k in range(16)]
        ysq = [c.sb(f"c_ysq{i}", [128, CH], BF16, es=pes) for i in range(2)]
        onec = c.sb("c_onec", [128, 1], BF16, es=pes)
        rsy = c.sb("c_rsy", [1, CH], F32, es=pes)
        rsyc = c.sb("c_rsyc", [128, 4], F32, es=pes)
        t3r = [c.sb(f"c_t3r{i}", [128, 1024], F32, es=pes) for i in range(NB)]
        ub = [c.sb(f"c_ub{i}", [128, 1024], F32, es=pes) for i in range(NB)]
        ps = [pes.enter_context(nc.psum_tensor(f"c5_ps{i}", [128, 512], F32))
              for i in range(4)]
        ps_ss = pes.enter_context(nc.psum_tensor("c5_ss", [1, 512], F32))
        ps_rt = pes.enter_context(nc.psum_tensor("c5_rt", [128, 4], F32))
        c.dve.raw("memset", onec[:, :], 1.0)
        c.dve.drain()
        t_onec = c.dve.inc()
        c.te.wait(t_onec)
        c.te.wait(t_w)
        ld_tok = None
        yg_tok = None
        ss_tok = None
        rt_tok = None
        udma = [None] * NB
        bank_tok = [None] * 4
        for ci in range(NCHUNK):
            c.sync.wait(ld_tok)
            c.qa.begin()
            for k in range(16):
                c.qa.dma(yb[k][:, :], y_d[k, :, CH * ci:CH * (ci + 1)])
                c.qa.dma(zb[k][:, :], zxb_d[k, :, CH * ci:CH * (ci + 1)])
            t_ld = c.qa.end()
            mm_tok = [None, None]
            for k in range(16):
                si = k % 2
                c.act.wait(t_ld)
                c.act.wait(yg_tok)
                c.act.raw("activation", zs[si][:, :], zb[k][:, :], AF.Silu,
                          bias=0.0, scale=1.0)
                t_zs = c.act.inc()
                c.dve.wait(t_zs)
                c.dve.raw("tensor_tensor", yg[k][:, :], yb[k][:, :],
                          zs[si][:, :], ALU.mult)
                c.dve.drain()
                t_yg = c.dve.inc()
                c.act.wait(t_yg)
                c.act.wait(mm_tok[si])
                c.act.raw("activation", ysq[si][:, :], yg[k][:, :], AF.Square,
                          bias=0.0, scale=1.0)
                t_sq = c.act.inc()
                last_sq = t_sq
                c.te.wait(t_sq)
                c.te.wait(ss_tok)
                c.te.raw("matmul", ps_ss[:, :], onec[:, :], ysq[si][:, :],
                         start=(k == 0), stop=(k == 15))
                t_ssm = c.te.inc()
                mm_tok[si] = t_ssm
            yg_tok = t_ssm
            c.act.wait(t_ssm)
            c.act.wait(rt_tok)
            epst, epstok = c.const_col(1e-5)
            c.act.wait(epstok)
            c.act.raw("activation", rsy[0:1, :], ps_ss[0:1, :], AF.Sqrt,
                      bias=epst[0:1, 0:1], scale=1.0 / 2048.0)
            t_rs = c.act.inc()
            ss_tok = t_rs
            c.dve.wait(t_rs)
            c.dve.raw("reciprocal", rsy[0:1, :], rsy[0:1, :])
            c.dve.drain()
            t_rr = c.dve.inc()
            c.te.wait(t_rr)
            for ti in range(4):
                c.te.raw("transpose", ps_rt[:, ti:ti + 1],
                         rsy[0:1, 128 * ti:128 * (ti + 1)], c.identf[0:1, 0:1])
            t_rtm = c.te.inc()
            c.dve.wait(t_rtm)
            c.dve.raw("tensor_copy", rsyc[:, :], ps_rt[:, 0:4])
            c.dve.drain()
            t_rsc = c.dve.inc()
            rt_tok = t_rsc
            for t_ in bank_tok:
                c.te.wait(t_)
            c.te.wait(last_sq)
            for ti in range(4):
                i = 4 * ci + ti
                bi = i % NB
                c.dve.wait(udma[bi])
                tl = c.qb.one(t3r[bi][:, :], t3_d[128 * i:128 * (i + 1), :])
                for jc in range(2):
                    pb_ = (2 * ti + jc) % 4
                    c.te.wait(bank_tok[pb_])
                    for k in range(16):
                        c.te.raw("matmul", ps[pb_][:, :],
                                 yg[k][:, 128 * ti:128 * (ti + 1)],
                                 outT[k][:, 512 * jc:512 * (jc + 1)],
                                 start=(k == 0), stop=(k == 15))
                    t_mm = c.te.inc()
                    c.dve.wait(t_mm)
                    c.dve.wait(tl)
                    c.dve.wait(t_rsc)
                    c.dve.raw("scalar_tensor_tensor",
                              ub[bi][:, 512 * jc:512 * (jc + 1)], ps[pb_][:, :],
                              rsyc[:, ti:ti + 1],
                              t3r[bi][:, 512 * jc:512 * (jc + 1)],
                              ALU.mult, ALU.add)
                    c.dve.drain()
                    bank_tok[pb_] = c.dve.inc()
                t_u = bank_tok[(2 * ti + 1) % 4]
                c.sync.wait(t_u)
                udma[bi] = c.qb.one(u_d[128 * i:128 * (i + 1), :], ub[bi][:, :])
            ld_tok = t_u
        c.barrier([u for u in udma if u is not None] + [bank_tok[i] for i in range(4) if bank_tok[i] is not None])

    with ExitStack() as pes:
        m2T = [c.sb(f"c_m2T{k}", [128, 1024], BF16, es=pes) for k in range(32)]
        c.qw.begin()
        for k in range(32):
            c.qw.dma(m2T[k][:, :], W["c_m2T"][128 * k:128 * (k + 1), :])
        t_w2r = c.qw.end()
        m1b = [c.sb(f"c_m1b{s}", [128, 8 * 512], BF16, es=pes) for s in range(2)]
        m1w = [[m1b[s][:, 512 * k:512 * (k + 1)] for k in range(8)]
               for s in range(2)]
        ubr = [c.sb(f"c_ubr{i}", [128, 1024], F32, es=pes) for i in range(4)]
        xbr = [c.sb(f"c_xbr{i}", [128, 1024], F32, es=pes) for i in range(2)]
        hb = [c.sb(f"c_hb{i}", [128, 1024], BF16, es=pes) for i in range(2)]
        b3b = [c.sb(f"c_b3b{i}", [128, 1024], F32, es=pes) for i in range(4)]
        x3b = [c.sb(f"c_x3b{i}", [128, 1024], F32, es=pes) for i in range(2)]
        lns = [mk_ln_scratch(c, f"C6{i}", es=pes) for i in range(2)]
        lns2 = [mk_ln_scratch(c, f"C62{i}", es=pes) for i in range(2)]
        hn_fm = [c.sb(f"c_hnfm{k}", [128, CH], BF16, es=pes) for k in range(8)]
        hm = [c.sb(f"c_hm{k}", [128, CH], BF16, es=pes) for k in range(32)]
        ps_tr = pes.enter_context(nc.psum_tensor("c6_pstr", [128, 2048], BF16))
        ps = [pes.enter_context(nc.psum_tensor(f"c6_ps{i}", [128, 512], F32))
              for i in range(4)]
        c.te.wait(t_w2r)
        otoks = [None] * 2
        uld = [None] * 4
        xld = [None] * 2
        bank_tok = [None] * 4
        m1_tok = [None, None]
        half_tok = [None, None]
        entry_tok = [None] * 2
        prev_chunk = None
        b3_tok = [None] * 4
        for ci in range(NCHUNK):
            for ti in range(4):
                i = 4 * ci + ti
                bi = i % 2
                c.sync.wait(uld[ti])
                c.sync.wait(entry_tok[bi])
                tlu = c.qb.one(ubr[ti][:, :], u_d[128 * i:128 * (i + 1), :])
                c.dve.wait(tlu)
                _, rstd2, _ = ln_stats2(c, ubr[ti][:, :], lns[bi], eps=1e-6,
                                        scale=1024.0)
                c.dve.raw("scalar_tensor_tensor", hb[bi][:, :], ubr[ti][:, :],
                          rstd2, n2bc[:, :], ALU.mult, ALU.mult)
                c.dve.drain()
                t_hb = c.dve.inc()
                hf = i % 2
                c.te.wait(t_hb)
                c.te.wait(half_tok[hf])
                c.te.wait(prev_chunk)
                for k in range(8):
                    c.te.raw("transpose",
                             ps_tr[:, hf * 1024 + k * 128:hf * 1024 + (k + 1) * 128],
                             hb[bi][:, k * 128:(k + 1) * 128], c.ident16[:, :])
                t_tr = c.te.inc()
                c.dve.wait(t_tr)
                for k in range(8):
                    c.dve.raw("tensor_copy", hn_fm[k][:, 128 * ti:128 * (ti + 1)],
                              ps_tr[:, hf * 1024 + k * 128:hf * 1024 + (k + 1) * 128])
                c.dve.drain()
                entry_tok[bi] = half_tok[hf] = c.dve.inc()
            for t in entry_tok:
                c.te.wait(t)
            for q in range(8):
                s = q % 2
                c.pool.wait(m1_tok[s])
                c.qw.begin()
                for k in range(8):
                    c.qw.dma(m1w[s][k][:, :],
                             W["c_m1T"][128 * k:128 * (k + 1),
                                        512 * q:512 * (q + 1)])
                t_m1 = c.qw.end()
                c.te.wait(t_m1)
                for jj in range(4):
                    j = 4 * q + jj
                    pb_ = jj % 4
                    c.te.wait(bank_tok[pb_])
                    for k in range(8):
                        c.te.raw("matmul", ps[pb_][:, :],
                                 m1w[s][k][:, jj * 128:(jj + 1) * 128],
                                 hn_fm[k][:, :], start=(k == 0), stop=(k == 7))
                    t_mm = c.te.inc()
                    c.act.wait(t_mm)
                    c.act.wait(m1_tok[s])
                    c.act.raw("activation", hm[j][:, :], ps[pb_][:, :],
                              AF.Gelu, bias=mb1c[:, j:j + 1], scale=1.0)
                    bank_tok[pb_] = c.act.inc()
                m1_tok[s] = bank_tok[(4 * q + 3) % 4]
            for t_ in bank_tok:
                c.te.wait(t_)
            for ti in range(4):
                for jc in range(2):
                    pb_ = (2 * ti + jc) % 4
                    c.te.wait(bank_tok[pb_])
                    for k in range(32):
                        c.te.raw("matmul", ps[pb_][:, :],
                                 hm[k][:, 128 * ti:128 * (ti + 1)],
                                 m2T[k][:, 512 * jc:512 * (jc + 1)],
                                 start=(k == 0), stop=(k == 31))
                    t_mm = c.te.inc()
                    c.dve.wait(t_mm)
                    c.dve.wait(b3_tok[ti])
                    c.dve.raw("tensor_tensor",
                              b3b[ti][:, 512 * jc:512 * (jc + 1)], ps[pb_][:, :],
                              mb2bc[:, 512 * jc:512 * (jc + 1)], ALU.add)
                    c.dve.drain()
                    c.dve.raw("tensor_tensor",
                              b3b[ti][:, 512 * jc:512 * (jc + 1)],
                              b3b[ti][:, 512 * jc:512 * (jc + 1)],
                              ubr[ti][:, 512 * jc:512 * (jc + 1)], ALU.add)
                    c.dve.drain()
                    bank_tok[pb_] = c.dve.inc()
            prev_chunk = bank_tok[(2 * 3 + 1) % 4]
            for ti in range(4):
                i = 4 * ci + ti
                bi = i % 2
                c.dve.wait(otoks[bi])
                c.dve.wait(xld[bi])
                tlx = c.qb.one(xbr[bi][:, :], x_in[128 * i:128 * (i + 1), :])
                mean, rstd, mrs = ln_stats(c, b3b[ti][:, :], lns2[bi])
                c.dve.wait(tlx)
                c.dve.raw("tensor_scalar", x3b[bi][:, :], xbr[bi][:, :], mrs,
                          None, ALU.subtract)
                c.dve.drain()
                c.dve.raw("scalar_tensor_tensor", x3b[bi][:, :], b3b[ti][:, :],
                          rstd, x3b[bi][:, :], ALU.mult, ALU.add)
                c.dve.drain()
                t_x3 = c.dve.inc()
                b3_tok[ti] = t_x3
                uld[ti] = t_x3
                xld[bi] = t_x3
                c.sync.wait(t_x3)
                otoks[bi] = c.qb.one(x_out[128 * i:128 * (i + 1), :],
                                     x3b[bi][:, :])
    return [t for t in otoks if t is not None]



def build_program(stages=("A",)):
    nc = bass.Bass("TRN2", target_bir_lowering=False, debug=False)
    es = ExitStack()
    x0 = nc.dram_tensor("x0", [T, D], F32, kind="ExternalInput").ap()
    out = nc.dram_tensor("out", [T, D], F32, kind="ExternalOutput").ap()
    W = {}

    def win(name, shape):
        W[name] = nc.dram_tensor(name, list(shape), F32, kind="ExternalInput").ap()

    with es:
        c = Ctx(nc, es)
        build_ident(c)
        cur_in = x0
        final = None
        nstage = len(stages)
        sidx = 0
        xs = [nc.dram_tensor(f"xs{i}", [T, D], F32, kind="Internal").ap()
              for i in range(nstage - 1)]

        def nxt():
            nonlocal sidx
            o = out if sidx == nstage - 1 else xs[sidx]
            sidx += 1
            return o

        W16 = {}
        W16D = {}
        if "D" in stages:
            win("d_nw", [1024]); win("d_gateT", [1024, 8])
            win("d_w1T", [8 * 1024, 2048]); win("d_b1", [8, 2048])
            win("d_w2T", [8 * 2048, 1024]); win("d_b2", [8, 1024])
            W16D["w1T"] = nc.dram_tensor("d16_w1T", [8 * 1024, 2048], BF16,
                                         kind="Internal").ap()
            W16D["w2T"] = nc.dram_tensor("d16_w2T", [8 * 2048, 1024], BF16,
                                         kind="Internal").ap()
            stg_toks_d = stage_weights_bf16(c, [
                (W["d_w1T"], W16D["w1T"], 8 * 1024, 2048),
                (W["d_w2T"], W16D["w2T"], 8 * 2048, 1024),
            ])
        if "B" in stages:
            win("bw1T", [1024, 4096]); win("bw2T", [1024, 4096])
            win("bh1T", [4096, 4096]); win("bh2T", [4096, 4096])
            win("bw3T", [4096, 1024])
            win("w1_s", [4096]); win("w2_s", [4096])
            win("h1_s", [4096]); win("h2_s", [4096]); win("w3_s", [1024])
            for nm, shp in (("w1T", [1024, 4096]), ("w2T", [1024, 4096]),
                            ("h1T", [4096, 4096]), ("h2T", [4096, 4096]),
                            ("w3T", [4096, 1024])):
                W16[nm] = nc.dram_tensor(f"b16_{nm}", shp, BF16,
                                         kind="Internal").ap()
            stg_toks = stage_weights_bf16(c, [
                (W["bw1T"], W16["w1T"], 1024, 4096),
                (W["bw2T"], W16["w2T"], 1024, 4096),
                (W["bh1T"], W16["h1T"], 4096, 4096),
                (W["bh2T"], W16["h2T"], 4096, 4096),
                (W["bw3T"], W16["w3T"], 4096, 1024),
            ])
        if "A" in stages:
            win("w_vT", [1024, 1024]); win("b_v", [1024])
            win("w_outT", [1024, 1024]); win("b_out", [1024])
            o = nxt()
            with ExitStack() as ses:
                final = stage_A(c, cur_in, o, W, ses)
                c.barrier(final)
                c.drain_queues()
            cur_in = o
        if "B" in stages:
            c.barrier(stg_toks)
            o = nxt()
            with ExitStack() as ses:
                final = stage_B(c, cur_in, o, W, W16, ses)
                c.barrier(final)
                c.drain_queues()
            cur_in = o
        if "C" in stages:
            win("c_n1w", [1024]); win("c_inpT", [1024, 4384])
            win("c_convw", [2304, 4]); win("c_convb", [2304])
            win("c_dtb", [32]); win("c_Alog", [32]); win("c_D", [32])
            win("c_mnw", [2048]); win("c_outT", [2048, 1024])
            win("c_n2w", [1024]); win("c_m1T", [1024, 4096])
            win("c_mb1", [4096]); win("c_m2T", [4096, 1024])
            win("c_mb2", [1024])
            o = nxt()
            with ExitStack() as ses:
                final = stage_C(c, cur_in, o, W, ses)
                c.barrier(final)
                c.drain_queues()
            cur_in = o
        if "D" in stages:
            c.barrier(stg_toks_d)
            o = nxt()
            with ExitStack() as ses:
                final = stage_D(c, cur_in, o, W, W16D, ses)
                c.barrier(final)
                c.drain_queues()
            cur_in = o
        for t in final:
            c.sync.wait(t)
        c.flush()
    return nc, list(W.keys())


def prep_weights(inputs, stages=("A",)):
    t = lambda a: np.ascontiguousarray(np.asarray(a).T)
    r = lambda a: np.ascontiguousarray(np.asarray(a))
    prep = {}
    if "A" in stages:
        v_rows = np.arange(3072).reshape(16, 3, 64)[:, 2, :].reshape(-1)
        prep["w_vT"] = np.ascontiguousarray(np.asarray(inputs["L0_attn_w_in"])[v_rows].T)
        prep["b_v"] = r(np.asarray(inputs["L0_attn_b_in"])[v_rows])
        prep["w_outT"] = t(inputs["L0_attn_w_out"])
        prep["b_out"] = r(inputs["L0_attn_b_out"])
    if "B" in stages:
        prep["bw1T"] = t(inputs["L0_ffn_w1_w"])
        prep["bw2T"] = t(inputs["L0_ffn_w2_w"])
        prep["bh1T"] = t(inputs["L0_ffn_h1_w"])
        prep["bh2T"] = t(inputs["L0_ffn_h2_w"])
        prep["bw3T"] = t(inputs["L0_ffn_w3_w"])
        prep["w1_s"] = r(inputs["L0_ffn_w1_s"])
        prep["w2_s"] = r(inputs["L0_ffn_w2_s"])
        prep["h1_s"] = r(inputs["L0_ffn_h1_s"])
        prep["h2_s"] = r(inputs["L0_ffn_h2_s"])
        prep["w3_s"] = r(inputs["L0_ffn_w3_s"])
    if "C" in stages:
        prep["c_n1w"] = r(inputs["L1_attn_norm1_w"])
        prep["c_inpT"] = t(inputs["L1_attn_in_proj_w"])
        prep["c_convw"] = r(inputs["L1_attn_conv_w"])
        prep["c_convb"] = r(inputs["L1_attn_conv_b"])
        prep["c_dtb"] = r(inputs["L1_attn_dt_bias"])
        prep["c_Alog"] = r(inputs["L1_attn_A_log"])
        prep["c_D"] = r(inputs["L1_attn_D"])
        prep["c_mnw"] = r(inputs["L1_attn_mnorm_w"])
        prep["c_outT"] = t(inputs["L1_attn_out_proj_w"])
        prep["c_n2w"] = r(inputs["L1_attn_norm2_w"])
        prep["c_m1T"] = t(inputs["L1_attn_mlp_w1"])
        prep["c_mb1"] = r(inputs["L1_attn_mlp_b1"])
        prep["c_m2T"] = t(inputs["L1_attn_mlp_w2"])
        prep["c_mb2"] = r(inputs["L1_attn_mlp_b2"])
    if "D" in stages:
        prep["d_nw"] = r(inputs["L1_ffn_norm_weight"])
        prep["d_gateT"] = t(inputs["L1_ffn_gate_w"])
        prep["d_w1T"] = np.ascontiguousarray(
            np.asarray(inputs["L1_ffn_w1"]).transpose(0, 2, 1).reshape(8 * 1024, 2048))
        prep["d_b1"] = r(inputs["L1_ffn_b1"])
        prep["d_w2T"] = np.ascontiguousarray(
            np.asarray(inputs["L1_ffn_w2"]).transpose(0, 2, 1).reshape(8 * 2048, 1024))
        prep["d_b2"] = r(inputs["L1_ffn_b2"])
    return prep


def run_stages(x_full, inputs, stages=("A",)):
    x = np.ascontiguousarray(np.asarray(x_full).reshape(16, 2048, 1024))
    shards = [np.ascontiguousarray(x[2 * i:2 * i + 2].reshape(T, D))
              for i in range(NCORES)]
    nc, wnames = build_program(stages=stages)
    prep = prep_weights(inputs, stages)
    in_maps = [dict({"x0": shards[i]}, **{n: prep[n] for n in wnames})
               for i in range(NCORES)]
    res = run_bass_kernel_spmd(nc, in_maps, core_ids=list(range(NCORES)))
    return np.concatenate(
        [r["out"].reshape(2, 2048, 1024) for r in res.results], axis=0)


def kernel(**inputs):
    return run_stages(inputs["L0_attn_x"], inputs,
                      stages=("A", "B", "C", "D"))
